# version 02
# baseline (speedup 1.0000x reference)
import numpy as np
import concourse.bass as bass
import concourse.mybir as mybir
from concourse.bass_utils import run_bass_kernel_spmd
from concourse.bass_core_rust import IndirectOffsetOnAxis

F32 = mybir.dt.float32
U32 = mybir.dt.uint32
AF = mybir.ActivationFunctionType
ALU = mybir.AluOpType
AX = mybir.AxisListType

H = 2048
HC = 16
DH = 64
NH = 32
KV = 8
BLK = 512
NU = 9
FF = 8192
SCALE = DH ** -0.5

TWO_PI = 2.0 * np.pi
C1 = float(np.float32(TWO_PI)); C2 = float(np.float32(TWO_PI - np.float64(np.float32(TWO_PI))))
LN1E4 = np.log(10000.0)
L1 = float(np.float32(LN1E4)); L2 = float(np.float32(LN1E4 - np.float64(np.float32(LN1E4))))
HP1 = float(np.float32(np.pi / 2)); HP2 = float(np.float32(np.pi / 2 - np.float64(np.float32(np.pi / 2))))

ENGINES = ("sync", "tensor", "vector", "scalar", "gpsimd")
ESEM = {"vector": "ev", "scalar": "sc", "tensor": "mm", "gpsimd": "gc", "sync": "sq"}


class Prog:
    def __init__(self):
        self.ops = {e: [] for e in ENGINES}
        self.cnt = {}
        self.bwr = {}
        self.brd = {}

    def w(self, eng, sem, val):
        if val and val > 0:
            self.ops[eng].append(lambda h, sems, s=sem, v=val: h.wait_ge(sems[s], v))

    def do(self, eng, build, sem=None, amt=1):
        if eng in ("vector", "scalar") and sem in ("ev", "sc"):
            self.ops[eng].append(lambda h, sems: h.drain())
        if sem is None:
            self.ops[eng].append(lambda h, sems, b=build: b(h, sems))
            return None
        self.cnt[sem] = self.cnt.get(sem, 0) + amt
        def fn(h, sems, b=build, s=sem, a=amt):
            b(h, sems).then_inc(sems[s], a)
        self.ops[eng].append(fn)
        return self.cnt[sem]

    def c(self, sem):
        return self.cnt.get(sem, 0)

    def wl(self, eng, *lanes):
        for ln in lanes:
            self.w(eng, ln, self.c(ln))

    def drain(self, eng):
        self.ops[eng].append(lambda h, sems: h.drain())

    def barrier(self):
        snap = dict(self.cnt)
        for e in ENGINES:
            for s, v in snap.items():
                self.w(e, s, v)
        self.bwr = {}; self.brd = {}

    def bank_w(self, b):
        g = self.brd.get(b)
        if g:
            self.w("tensor", g[0], g[1])

    def bank_wrote(self, b):
        self.bwr[b] = ("mm", self.c("mm"))

    def bank_r(self, eng, b):
        g = self.bwr.get(b)
        if g:
            self.w(eng, g[0], g[1])

    def bank_read(self, eng, b):
        self.brd[b] = (ESEM[eng], self.c(ESEM[eng]))


def build_program(debug=False):
    nc = bass.Bass("TRN2", target_bir_lowering=False)
    d = {}
    def din(name, shape, dt=F32):
        d[name] = nc.dram_tensor(name, list(shape), dt, kind="ExternalInput").ap()
        return d[name]

    x_own = din("x_own", [1024, H])
    xkeys = din("xkeys", [NU * BLK, H])
    Wq = din("Wq", [H, H]); Wk = din("Wk", [512, H]); Wv = din("Wv", [512, H]); Wo = din("Wo", [H, H])
    Wg = din("Wg", [H, FF]); Wu = din("Wu", [H, FF]); Wd = din("Wd", [FF, H])
    bqc = din("bqc", [128, 16]); bkc = din("bkc", [128, 4]); bvc = din("bvc", [64, 8]); boc = din("boc", [128, 16])
    kwpair = din("kwpair", [2, 16])
    ident = din("ident", [128, 128]); Pi = din("Pi", [128, 128]); E2 = din("E2", [2, 128])
    onesr = din("onesr", [1, 128]); onesc = din("onesc", [128, 1]); ones2 = din("ones2", [128, 2])
    bc65 = din("bc65", [65, 64])
    tri = din("tri", [128, 128])
    invfh = din("invfh", [64, 1]); invfl = din("invfl", [64, 1]); tb = din("tb", [64, 4096])
    nsin = din("nsin", [64, 4096]); ncos = din("ncos", [64, 4096])
    wselr = din("wselr", [1, 2 * NU])
    qgidx = din("qgidx", [128, NU * KV], U32)
    tabidx = din("tabidx", [128, NU + 2], U32)

    y = nc.dram_tensor("y", [1024, H], F32, kind="ExternalOutput").ap()

    def dscr(name, shape):
        if debug:
            return nc.dram_tensor(name, list(shape), F32, kind="ExternalOutput").ap()
        return nc.dram_tensor(name, list(shape), F32).ap()

    qpad = dscr("qpad", [2 * KV * 128, 4 * BLK])
    ksp = dscr("ksp", [NU, 8 * 128, BLK])
    q2d = dscr("q2d", [2, 16, 2, BLK])
    vsp = dscr("vsp", [NU, 4 * 128, 520])
    cosd = dscr("cosd", [128, 4096])
    sind = dscr("sind", [128, 4096])
    xTsp = dscr("xTsp", [128, HC * 1024])
    if debug:
        d_oA = nc.dram_tensor("d_oA", [65, 16384], F32, kind="ExternalOutput").ap()
        d_oB = nc.dram_tensor("d_oB", [65, 16384], F32, kind="ExternalOutput").ap()
        d_ncr = nc.dram_tensor("d_ncr", [1, 32], F32, kind="ExternalOutput").ap()
        d_mq = nc.dram_tensor("d_mq", [1, 32], F32, kind="ExternalOutput").ap()
        d_mk = nc.dram_tensor("d_mk", [1, 8], F32, kind="ExternalOutput").ap()
        d_onT = nc.dram_tensor("d_onT", [128, 16384], F32, kind="ExternalOutput").ap()
        d_y1 = nc.dram_tensor("d_y1", [128, 16384], F32, kind="ExternalOutput").ap()
        d_x1 = nc.dram_tensor("d_x1", [128, 16384], F32, kind="ExternalOutput").ap()
        d_h0 = nc.dram_tensor("d_h0", [128, 512], F32, kind="ExternalOutput").ap()
        d_qt = nc.dram_tensor("d_qt", [128, 1536], F32, kind="ExternalOutput").ap()
        d_cr = nc.dram_tensor("d_cr", [1, 8192], F32, kind="ExternalOutput").ap()
        d_kk = nc.dram_tensor("d_kk", [128, 3072], F32, kind="ExternalOutput").ap()
        d_kn = nc.dram_tensor("d_kn", [NU, 4, 2, 512], F32, kind="ExternalOutput").ap()
        d_o2 = nc.dram_tensor("d_o2", [128, 8192], F32, kind="ExternalOutput").ap()

    P = Prog()

    import contextlib
    _es = contextlib.ExitStack()
    with _es:
        AR1 = _es.enter_context(nc.sbuf_tensor("AR1", [128, 16384], F32))
        AR2 = _es.enter_context(nc.sbuf_tensor("AR2", [128, 16384], F32))
        AR3 = _es.enter_context(nc.sbuf_tensor("AR3", [128, 16384], F32))
        AR4 = _es.enter_context(nc.sbuf_tensor("AR4", [128, 3472], F32))
        IDX = _es.enter_context(nc.sbuf_tensor("IDX", [128, 96], U32))
        PS = _es.enter_context(nc.psum_tensor("PS", [128, 4096], F32))
        _lanes = ["mm", "ev", "sc", "boot", "boot2", "tsp", "qtg", "qz", "xa", "xb", "wq",
                  "q2s", "qsp", "utg", "sp", "pi", "crl", "cw", "ks", "va", "vb", "qa", "qb",
                  "os", "xc", "xd", "xsp", "fw", "st", "dbg"]
        _semh = {n: _es.enter_context(nc.semaphore("s_" + n)) for n in _lanes}
        block = _es.enter_context(nc.Block())
        sems = _semh

        def pb(i, lo=0, hi=512, p0=0, p1=128):
            return PS[p0:p1, i * 512 + lo:i * 512 + hi]

        A4 = AR4
        identS = lambda: A4[:, 0:128]
        PiS = lambda: A4[:, 128:256]
        triS = lambda: A4[:, 256:384]
        onesrS = lambda: A4[0:1, 384:512]
        onescS = lambda: A4[:, 512:513]
        ones2S = lambda: A4[:, 514:516]
        E2S = lambda: A4[0:2, 516:644]
        kwpS = lambda: A4[0:2, 644:660]
        bqS = lambda: A4[:, 660:676]
        bkS = lambda: A4[:, 676:680]
        bvS = lambda: A4[0:64, 680:688]
        boS = lambda: A4[:, 688:704]
        kwcolS = lambda: A4[:, 704:720]
        ccolS = lambda: A4[:, 720:752]
        wcolS = lambda: A4[:, 752:770]
        Mq2S = lambda: A4[0:1, 770:802]
        Mk2S = lambda: A4[0:1, 802:810]
        tmp2S = lambda: A4[0:2, 810:812]
        c2S = lambda: A4[0:1, 814:846]
        crS = lambda: A4[0:1, 846:878]
        invfhS = lambda: A4[0:64, 878:879]
        invflS = lambda: A4[0:64, 879:880]
        wsS = lambda: A4[0:1, 884:902]
        qcosS = lambda ch: A4[:, 1024 + ch * 1024:1024 + ch * 1024 + 512]
        qsinS = lambda ch: A4[:, 1536 + ch * 1024:1536 + ch * 1024 + 512]
        q2stgS = lambda: AR3[0:2, 8192:8704]
        ucosS = lambda: A4[:, 1024:1536]
        usinS = lambda: A4[:, 1536:2048]

        for (dst, src) in [(identS(), ident), (PiS(), Pi), (triS(), tri),
                           (onesrS(), onesr), (onescS(), onesc), (ones2S(), ones2),
                           (E2S(), E2), (kwpS(), kwpair), (bqS(), bqc), (bkS(), bkc),
                           (bvS(), bvc), (boS(), boc), (wsS(), wselr),
                           (invfhS(), invfh), (invflS(), invfl)]:
            P.do("sync", lambda h, s, a=dst, b=src: h.dma_start(out=a, in_=b), "boot", 16)
        P.do("sync", lambda h, s: h.dma_start(out=IDX[:, 0:NU * KV], in_=qgidx), "boot", 16)
        P.do("sync", lambda h, s: h.dma_start(out=IDX[:, 72:72 + NU + 2], in_=tabidx), "boot", 16)
        tbS = AR2[0:64, 0:4096]
        nsS = AR2[0:64, 4096:8192]
        ncS = AR2[0:64, 8192:12288]
        angS = AR2[0:64, 12288:16384]
        redsS = AR3[0:64, 0:4096]
        redcS = AR3[0:64, 4096:8192]
        cosS = AR3[0:64, 8192:12288]
        sinS_ = AR3[0:64, 12288:16384]
        P.do("sync", lambda h, s: h.dma_start(out=tbS, in_=tb), "boot", 16)
        P.do("sync", lambda h, s: h.dma_start(out=nsS, in_=nsin), "boot", 16)
        P.do("sync", lambda h, s: h.dma_start(out=ncS, in_=ncos), "boot", 16)
        for e in ("vector", "scalar", "tensor"):
            P.wl(e, "boot")
        P.do("vector", lambda h, s: h.tensor_scalar_mul(angS, tbS, invfhS()), "ev")
        P.do("vector", lambda h, s: h.scalar_tensor_tensor(angS, tbS, invflS(), angS, ALU.mult, ALU.add), "ev")
        P.do("vector", lambda h, s: h.scalar_tensor_tensor(redsS, nsS, -C1, angS, ALU.mult, ALU.add), "ev")
        P.do("vector", lambda h, s: h.scalar_tensor_tensor(redsS, nsS, -C2, redsS, ALU.mult, ALU.add), "ev")
        P.do("vector", lambda h, s: h.scalar_tensor_tensor(redcS, ncS, -C1, angS, ALU.mult, ALU.add), "ev")
        P.do("vector", lambda h, s: h.scalar_tensor_tensor(redcS, ncS, -C2, redcS, ALU.mult, ALU.add), "ev")
        P.do("vector", lambda h, s: h.tensor_scalar_add(redcS, redcS, HP1), "ev")
        P.do("vector", lambda h, s: h.tensor_scalar_add(redcS, redcS, HP2), "ev")
        P.w("scalar", "ev", P.c("ev"))
        P.do("scalar", lambda h, s: h.activation(cosS, redcS, AF.Sin), "sc")
        P.do("scalar", lambda h, s: h.activation(sinS_[0:32, :], redsS[0:32, :], AF.Sin, scale=-1.0), "sc")
        P.do("scalar", lambda h, s: h.activation(sinS_[32:64, :], redsS[32:64, :], AF.Sin), "sc")
        s_tab = P.c("sc")
        P.w("sync", "sc", s_tab)
        P.do("sync", lambda h, s: h.dma_start(out=cosd[0:64, :], in_=cosS), "tsp", 16)
        P.do("sync", lambda h, s: h.dma_start(out=cosd[64:128, :], in_=cosS), "tsp", 16)
        P.do("sync", lambda h, s: h.dma_start(out=sind[0:64, :], in_=sinS_), "tsp", 16)
        P.do("sync", lambda h, s: h.dma_start(out=sind[64:128, :], in_=sinS_), "tsp", 16)

        P.bank_w(0)
        P.do("tensor", lambda h, s: h.matmul(PS[0:128, 0:16], E2S(), kwpS(), start=True, stop=True), "mm")
        P.do("tensor", lambda h, s: h.matmul(PS[0:128, 16:34], onesrS(), wsS(), start=True, stop=True), "mm")
        P.bank_wrote(0)
        P.bank_r("vector", 0)
        P.do("vector", lambda h, s: h.tensor_copy(kwcolS(), PS[0:128, 0:16]), "ev")
        P.do("vector", lambda h, s: h.tensor_copy(wcolS(), PS[0:128, 16:34]), "ev")
        P.bank_read("vector", 0)
        P.do("vector", lambda h, s: h.memset(Mk2S(), 0.0), "ev")

        cosd_v = cosd[:].rearrange("p (a b) -> (p a) b", b=512)
        sind_v = sind[:].rearrange("p (a b) -> (p a) b", b=512)
        P.wl("gpsimd", "tsp", "boot")
        for ch in range(2):
            P.do("gpsimd", lambda h, s, ch=ch: h.indirect_dma_start(
                qcosS(ch), None, cosd_v, IndirectOffsetOnAxis(ap=IDX[:, 72 + NU + ch:73 + NU + ch], axis=0)), "qtg", 16)
            P.do("gpsimd", lambda h, s, ch=ch: h.indirect_dma_start(
                qsinS(ch), None, sind_v, IndirectOffsetOnAxis(ap=IDX[:, 72 + NU + ch:73 + NU + ch], axis=0)), "qtg", 16)
        qtab_done = None

        zb = AR1[:, 0:2048]
        P.do("vector", lambda h, s: h.memset(zb, 0.0), "ev")
        P.w("gpsimd", "ev", P.c("ev"))
        for blk_ in range(16):
            P.do("gpsimd", lambda h, s, b=blk_: h.dma_start(out=qpad[b * 128:(b + 1) * 128, :], in_=zb), "qz", 16)
        qpad_zeroed = None

        def strip16(src_fn, dst_fn, banks=(4, 5), evac_eng="vector"):
            for g in range(4):
                b = banks[g % 2]
                P.bank_w(b)
                for j in range(4):
                    P.do("tensor", lambda h, s, src=src_fn(g * 4 + j), b=b, j=j:
                         h.transpose(pb(b, j * 128, (j + 1) * 128), src, identS()), "mm")
                P.bank_wrote(b)
                P.bank_r(evac_eng, b)
                P.do(evac_eng, lambda h, s, dst=dst_fn(g), b=b:
                     (h.tensor_copy(dst, pb(b)) if evac_eng == "vector" else h.copy(dst, pb(b))),
                     ESEM[evac_eng])
                P.bank_read(evac_eng, b)

        xstg = lambda i: AR3[:, (i % 2) * 2048:(i % 2) * 2048 + 2048]
        wraw = AR3[:, 4096:6144]
        wTb = AR3[:, 6144:8192]
        qsS = AR3[:, 10240:10752]
        m1S = AR3[:, 10752:11264]
        m2S = AR3[:, 11264:11776]
        qtS = AR3[:, 11776:12288]
        sqS = AR3[:, 12288:12800]

        P.w("sync", "sc", s_tab)
        P.wl("scalar", "tsp")
        P.wl("vector", "qz")
        xT3 = AR1[:].rearrange("p (hc t) -> p hc t", t=1024)
        x_ld = []
        for tt in range(8):
            P.w("sync", "mm", P.c("mm")) if tt >= 2 else None
            xl_ = "xa" if tt % 2 == 0 else "xb"
            P.do("sync", lambda h, s, tt=tt: h.dma_start(out=xstg(tt), in_=x_own[tt * 128:(tt + 1) * 128, :]), xl_, 16)
            P.wl("tensor", xl_)
            strip16(lambda j, tt=tt: xstg(tt)[:, j * 128:(j + 1) * 128],
                    lambda g, tt=tt: xT3[:, g * 4:(g + 1) * 4, tt * 128:(tt + 1) * 128])
        xT_done_ev = P.c("ev")

        P.wl("vector", "qtg")
        wq_ld = []
        P.do("sync", lambda h, s: h.dma_start(out=wraw, in_=Wq[0:128, :]), "wq", 16)
        wT3 = None
        last_qspill_gq = 0
        last_norm_mm = 0
        last_q_square = 0
        for dqc in range(16):
            P.wl("tensor", "wq")
            strip16(lambda j: wraw[:, j * 128:(j + 1) * 128],
                    lambda g: wTb[:, g * 512:(g + 1) * 512])
            wqT_done = P.c("ev")
            P.w("sync", "mm", P.c("mm"))
            if dqc + 1 < 16:
                P.do("sync", lambda h, s, dqc=dqc: h.dma_start(out=wraw, in_=Wq[(dqc + 1) * 128:(dqc + 2) * 128, :]), "wq", 16)
            for ch in range(2):
                b = ch
                rb = 2 + ch
                P.w("tensor", "ev", wqT_done)
                P.bank_w(b)
                for hc in range(HC):
                    P.do("tensor", lambda h, s, hc=hc, b=b, ch=ch:
                         h.matmul(pb(b), wTb[:, hc * 128:(hc + 1) * 128],
                                  AR1[:, hc * 1024 + ch * 512:hc * 1024 + ch * 512 + 512],
                                  start=(hc == 0), stop=(hc == HC - 1)), "mm")
                P.bank_wrote(b)
                P.bank_r("vector", b)
                P.w("vector", "mm", P.c("mm"))
                P.do("vector", lambda h, s, dqc=dqc, b=b:
                     h.tensor_scalar(qsS, pb(b), bqS()[:, dqc:dqc + 1], kwcolS()[:, dqc:dqc + 1], ALU.add, ALU.mult), "ev")
                P.bank_read("vector", b)
                ev_qs = P.c("ev")
                P.w("tensor", "ev", ev_qs)
                P.bank_w(rb)
                P.do("tensor", lambda h, s, rb=rb: h.matmul(pb(rb), PiS(), qsS, start=True, stop=True), "mm")
                P.bank_wrote(rb)
                mm_rot = P.c("mm")
                P.do("vector", lambda h, s, ch=ch: h.tensor_mul(m1S, qsS, qcosS(ch)), "ev")
                P.bank_r("vector", rb)
                P.do("vector", lambda h, s, rb=rb, ch=ch: h.tensor_mul(m2S, pb(rb), qsinS(ch)), "ev")
                P.bank_read("vector", rb)
                P.wl("vector", "qsp")
                P.w("vector", "sc", last_q_square)
                P.do("vector", lambda h, s: h.tensor_add(qtS, m1S, m2S), "ev")
                ev_qt = P.c("ev")
                P.w("gpsimd", "ev", ev_qt)
                for hh in range(2):
                    hq = 2 * dqc + hh
                    kvh = hq // 4
                    qh = hq % 4
                    par = kvh % 2
                    r0 = ch * 1024 + kvh * 128 + par * 64
                    P.do("gpsimd", lambda h, s, r0=r0, qh=qh, hh=hh:
                         h.dma_start(out=qpad[r0:r0 + 64, qh * 512:(qh + 1) * 512],
                                     in_=qtS[hh * 64:(hh + 1) * 64, :]), "qsp", 16)

                P.w("scalar", "ev", ev_qt)
                P.w("scalar", "mm", last_norm_mm)
                P.do("scalar", lambda h, s: h.activation(sqS, qtS, AF.Square), "sc")
                sc_sq = P.c("sc")
                last_q_square = sc_sq

                P.w("tensor", "sc", sc_sq)
                P.bank_w(6)
                P.do("tensor", lambda h, s: h.matmul(pb(6, 0, 512, 0, 2), ones2S(), sqS, start=True, stop=True), "mm")
                P.bank_wrote(6)
                last_norm_mm = P.c("mm")
                P.bank_r("vector", 6)
                P.wl("vector", "q2s")
                P.do("vector", lambda h, s: h.tensor_copy(q2stgS(), pb(6, 0, 512, 0, 2)), "ev")
                P.bank_read("vector", 6)
                ev_q2 = P.c("ev")
                P.w("sync", "ev", ev_q2)
                P.do("sync", lambda h, s, ch=ch, dqc=dqc: h.dma_start(out=q2d[ch, dqc], in_=q2stgS()), "q2s", 16)



        P.w("sync", "ev", xT_done_ev)
        P.do("sync", lambda h, s: h.dma_start(out=xTsp[:], in_=AR1[:]), "xsp", 16)

        P.barrier()
        WkT3 = AR2[:].rearrange("p (hc q) -> p hc q", q=512)
        for i in range(4):
            P.w("sync", "mm", P.c("mm")) if i > 0 else None
            P.do("sync", lambda h, s, i=i: h.dma_start(out=AR1[:, 0:2048], in_=Wk[i * 128:(i + 1) * 128, :]), "wq", 16)
            P.wl("tensor", "wq")
            strip16(lambda j: AR1[:, j * 128:(j + 1) * 128],
                    lambda g, i=i: WkT3[:, g * 4:(g + 1) * 4, i * 128:(i + 1) * 128])
        for i in range(4):
            P.w("sync", "mm", P.c("mm"))
            P.do("sync", lambda h, s, i=i: h.dma_start(out=AR1[:, 0:2048], in_=Wv[i * 128:(i + 1) * 128, :]), "wq", 16)
            P.wl("tensor", "wq")
            strip16(lambda j: AR1[:, j * 128:(j + 1) * 128],
                    lambda g, i=i: WkT3[:, 16 + g * 4:16 + (g + 1) * 4, i * 128:(i + 1) * 128])
        wkvT_done = P.c("ev")
        WkTs = lambda hc, dkc: AR2[:, hc * 512 + dkc * 128:hc * 512 + (dkc + 1) * 128]
        WvTs = lambda hc: AR2[:, 8192 + hc * 512:8192 + (hc + 1) * 512]

        kscr = AR3[:, 0:512]
        km1 = AR3[:, 512:1024]
        km2 = AR3[:, 1024:1536]
        ktil = AR3[:, 1536:2048]
        ksq = AR3[:, 2048:2560]
        vscr = lambda kt: AR3[:, 2560 + kt * 520:2560 + (kt + 1) * 520]
        vscr3 = lambda kt: AR3[:, 2560 + kt * 520:2560 + (kt + 1) * 520].rearrange("p (kv e) -> p kv e", e=65)
        P.wl("vector", "xa", "xb", "wq")
        for kt in range(4):
            P.do("vector", lambda h, s, kt=kt: h.memset(vscr3(kt)[:, :, 64:65], 1.0), "ev")

        padb = AR3[:, 7200:7712]
        pones = AR3[0:1, 7712:8224]
        P.do("vector", lambda h, s: h.memset(padb, 0.0), "ev")
        P.do("vector", lambda h, s: h.memset(pones, 1.0), "ev")
        ev_pad = P.c("ev")
        P.w("gpsimd", "ev", ev_pad)
        for u_ in range(NU):
            for kv_ in range(KV):
                if kv_ % 2 == 0:
                    P.do("gpsimd", lambda h, s, u_=u_, kv_=kv_: h.dma_start(
                        out=ksp[u_, kv_ * 128 + 64:kv_ * 128 + 65, :], in_=pones), "pi", 16)
                    P.do("gpsimd", lambda h, s, u_=u_, kv_=kv_: h.dma_start(
                        out=ksp[u_, kv_ * 128 + 65:kv_ * 128 + 128, :], in_=padb[0:63, :]), "pi", 16)
                else:
                    P.do("gpsimd", lambda h, s, u_=u_, kv_=kv_: h.dma_start(
                        out=ksp[u_, kv_ * 128:kv_ * 128 + 63, :], in_=padb[0:63, :]), "pi", 16)
                    P.do("gpsimd", lambda h, s, u_=u_, kv_=kv_: h.dma_start(
                        out=ksp[u_, kv_ * 128 + 63:kv_ * 128 + 64, :], in_=pones), "pi", 16)
        xkT = lambda hc: AR1[:, 4096 + hc * 512:4096 + (hc + 1) * 512]
        xkT3 = AR1[:, 4096:12288].rearrange("p (hc t) -> p hc t", t=512)
        last_ksp_gq = 0
        last_vsp_gq = 0
        last_knorm_mm = 0
        last_k_square = 0
        for u in range(NU):
            P.wl("gpsimd", "tsp")
            P.w("gpsimd", "ev", P.c("ev")) if u > 0 else None
            P.do("gpsimd", lambda h, s, u=u: h.indirect_dma_start(
                ucosS(), None, cosd_v, IndirectOffsetOnAxis(ap=IDX[:, 72 + u:73 + u], axis=0)), "utg", 16)
            P.do("gpsimd", lambda h, s, u=u: h.indirect_dma_start(
                usinS(), None, sind_v, IndirectOffsetOnAxis(ap=IDX[:, 72 + u:73 + u], axis=0)), "utg", 16)
            utab = None

            for tt in range(4):
                P.w("sync", "mm", P.c("mm"))
                P.do("sync", lambda h, s, u=u, tt=tt: h.dma_start(
                    out=AR1[:, (tt % 2) * 2048:(tt % 2) * 2048 + 2048],
                    in_=xkeys[u * 512 + tt * 128:u * 512 + (tt + 1) * 128, :]), "xa" if tt % 2 == 0 else "xb", 16)
                P.wl("tensor", "xa" if tt % 2 == 0 else "xb")
                strip16(lambda j, tt=tt: AR1[:, (tt % 2) * 2048 + j * 128:(tt % 2) * 2048 + (j + 1) * 128],
                        lambda g, tt=tt: xkT3[:, g * 4:(g + 1) * 4, tt * 128:(tt + 1) * 128])
            xkT_ev = P.c("ev")

            P.w("tensor", "ev", xkT_ev)
            P.wl("vector", "utg")
            for dkc in range(4):
                b = dkc % 2
                P.bank_w(b)
                for hc in range(HC):
                    P.do("tensor", lambda h, s, hc=hc, dkc=dkc, b=b:
                         h.matmul(pb(b), WkTs(hc, dkc), xkT(hc), start=(hc == 0), stop=(hc == HC - 1)), "mm")
                P.bank_wrote(b)
                P.bank_r("vector", b)
                P.w("vector", "mm", P.c("mm"))
                P.do("vector", lambda h, s, dkc=dkc, b=b:
                     h.tensor_scalar_add(kscr, pb(b), bkS()[:, dkc:dkc + 1]), "ev")
                P.bank_read("vector", b)
                ev_k = P.c("ev")
                P.w("tensor", "ev", ev_k)
                P.bank_w(2)
                P.do("tensor", lambda h, s: h.matmul(pb(2), PiS(), kscr, start=True, stop=True), "mm")
                P.bank_wrote(2)
                P.do("vector", lambda h, s: h.tensor_mul(km1, kscr, ucosS()), "ev")
                P.bank_r("vector", 2)
                P.do("vector", lambda h, s: h.tensor_mul(km2, pb(2), usinS()), "ev")
                P.bank_read("vector", 2)
                P.wl("vector", "sp")
                P.w("vector", "sc", last_k_square)
                P.do("vector", lambda h, s: h.tensor_add(ktil, km1, km2), "ev")
                ev_kt = P.c("ev")
                P.w("gpsimd", "ev", ev_kt)
                P.do("gpsimd", lambda h, s, u=u, dkc=dkc: h.dma_start(
                    out=ksp[u, (2 * dkc) * 128:(2 * dkc) * 128 + 64, :], in_=ktil[0:64, :]), "sp", 16)
                P.do("gpsimd", lambda h, s, u=u, dkc=dkc: h.dma_start(
                    out=ksp[u, (2 * dkc + 1) * 128 + 64:(2 * dkc + 1) * 128 + 128, :], in_=ktil[64:128, :]), "sp", 16)

                P.w("scalar", "ev", ev_kt)
                P.w("scalar", "mm", last_knorm_mm)
                P.do("scalar", lambda h, s: h.activation(ksq, ktil, AF.Square), "sc")
                sc_k = P.c("sc")
                last_k_square = sc_k
                P.w("tensor", "sc", sc_k)
                P.bank_w(6)
                P.do("tensor", lambda h, s: h.matmul(pb(6, 0, 512, 0, 2), ones2S(), ksq, start=True, stop=True), "mm")
                P.bank_wrote(6)
                P.bank_r("vector", 6)
                P.do("vector", lambda h, s: h.reduce_max(tmp2S()[0:2, 1:2], pb(6, 0, 512, 0, 2), axis=AX.X), "ev")
                P.bank_read("vector", 6)
                P.w("tensor", "ev", P.c("ev"))
                P.bank_w(7)
                P.do("tensor", lambda h, s: h.transpose(pb(7, 0, 2, 0, 1), tmp2S()[0:2, 1:2], identS()[0:2, 0:2]), "mm")
                P.bank_wrote(7)
                last_knorm_mm = P.c("mm")
                P.bank_r("vector", 7)
                P.do("vector", lambda h, s, dkc=dkc: h.tensor_max(Mk2S()[0:1, 2 * dkc:2 * dkc + 2],
                                                                  Mk2S()[0:1, 2 * dkc:2 * dkc + 2],
                                                                  pb(7, 0, 2, 0, 1)), "ev")
                P.drain("vector")
                P.bank_read("vector", 7)
            for kt in range(4):
                b = kt % 2
                P.bank_w(b)
                for hc in range(HC):
                    P.do("tensor", lambda h, s, hc=hc, kt=kt, b=b:
                         h.matmul(pb(b), xkT(hc)[:, kt * 128:(kt + 1) * 128], WvTs(hc),
                                  start=(hc == 0), stop=(hc == HC - 1)), "mm")
                P.bank_wrote(b)
                P.bank_r("vector", b)
                P.wl("vector", "sp")
                P.do("vector", lambda h, s, kt=kt, b=b:
                     h.tensor_copy(vscr3(kt)[:, :, 0:64], pb(b).rearrange("p (kv dv) -> p kv dv", dv=64)), "ev")
                P.bank_read("vector", b)
                ev_v = P.c("ev")
                P.w("gpsimd", "ev", ev_v)
                P.do("gpsimd", lambda h, s, u=u, kt=kt: h.dma_start(out=vsp[u, kt * 128:(kt + 1) * 128, :], in_=vscr(kt)), "sp", 16)



        P.drain("vector")
        crow = lambda: A4[0:1, 1024:3072]
        P.wl("sync", "q2s", "qsp", "utg", "qtg")
        for ch in range(2):
            for kv in range(KV):
                P.w("sync", "ev", P.c("ev"))
                P.w("sync", "sc", P.c("sc"))
                P.wl("sync", "cw")
                P.do("sync", lambda h, s, ch=ch, kv=kv: h.dma_start(
                    out=crow(), in_=q2d[ch, 2 * kv:2 * kv + 2]), "crl", 16)
                P.wl("vector", "crl")
                P.do("vector", lambda h, s, kv=kv: h.tensor_single_scalar(
                    crow(), crow(), Mk2S()[0:1, kv:kv + 1], ALU.mult), "ev")
                P.drain("vector")
                ev_c1 = P.c("ev")
                P.w("scalar", "ev", ev_c1)
                P.do("scalar", lambda h, s: h.activation(crow(), crow(), AF.Sqrt), "sc")
                sc_c = P.c("sc")
                P.w("vector", "sc", sc_c)
                P.do("vector", lambda h, s: h.tensor_scalar_add(crow(), crow(), -70.0), "ev")
                P.drain("vector")
                P.do("vector", lambda h, s: h.tensor_scalar_max(crow(), crow(), 0.0), "ev")
                P.drain("vector")
                P.do("vector", lambda h, s: h.tensor_scalar_mul(crow(), crow(), -1.0), "ev")
                P.drain("vector")
                ev_c2 = P.c("ev")
                P.w("gpsimd", "ev", ev_c2)
                r_c = 64 if kv % 2 == 0 else 63
                P.do("gpsimd", lambda h, s, ch=ch, kv=kv, r_c=r_c: h.dma_start(
                    out=qpad[ch * 1024 + kv * 128 + r_c:ch * 1024 + kv * 128 + r_c + 1, :],
                    in_=crow()), "cw", 16)

        P.barrier()
        Kslot = AR1[:, 0:4096]
        Kc = lambda kv: Kslot[:, kv * 512:(kv + 1) * 512]
        Vslot = lambda i: AR1[:, 4096 + (i % 2) * 2080:4096 + (i % 2) * 2080 + 2080]
        Vc = lambda i, kt: Vslot(i)[:, kt * 520:(kt + 1) * 520]
        qslot = lambda i: AR1[:, 8256 + (i % 2) * 2048:8256 + (i % 2) * 2048 + 2048]
        Pbuf = lambda kt: AR1[:, 12352 + kt * 512:12352 + (kt + 1) * 512]
        oaccA = AR2[0:65, :]
        oaccB = AR3[0:65, :]

        P.do("vector", lambda h, s: h.memset(oaccA, 0.0), "ev")
        P.do("vector", lambda h, s: h.memset(oaccB, 0.0), "ev")
        oacc_init = P.c("ev")

        kspv = lambda u: ksp[u].rearrange("(kv p) k -> p kv k", p=128)
        vspv = lambda u: vsp[u].rearrange("(kt p) e -> p kt e", p=128)
        P.wl("sync", "sp", "pi", "cw")
        P.do("sync", lambda h, s: h.dma_start(out=Kslot, in_=kspv(0)), "ks", 16)
        P.do("sync", lambda h, s: h.dma_start(out=Vslot(0), in_=vspv(0)), "va", 16)
        qpad_v = qpad[:]
        P.wl("gpsimd", "cw", "qz", "qsp")
        P.do("gpsimd", lambda h, s: h.indirect_dma_start(
            qslot(0), None, qpad_v, IndirectOffsetOnAxis(ap=IDX[:, 0:1], axis=0)), "qa", 16)

        exp_bank_sc = {6: 0, 7: 0}
        omm_of_p = {}
        stt_hist = []
        P.w("tensor", "ev", oacc_init)
        step = 0
        for u in range(NU):
            diag = (u >= 7)
            if u == 7:
                P.w("vector", "mm", P.c("mm"))
                for kt in range(1, 4):
                    P.do("vector", lambda h, s, kt=kt: h.memset(Pbuf(kt)[:, 0:kt * 128], 0.0), "ev")
                pz = P.c("ev")
                P.w("scalar", "ev", pz)
                P.w("tensor", "ev", pz)
            if u > 0:
                P.w("sync", "mm", P.c("mm"))
                P.do("sync", lambda h, s, u=u: h.dma_start(out=Kslot, in_=kspv(u)), "ks", 16)
            if u + 1 < NU:
                P.w("sync", "mm", P.c("mm")) if u > 0 else None
                P.do("sync", lambda h, s, u=u: h.dma_start(out=Vslot(u + 1), in_=vspv(u + 1)),
                     "va" if (u + 1) % 2 == 0 else "vb", 16)
            P.wl("tensor", "va" if u % 2 == 0 else "vb")
            P.wl("tensor", "ks")
            for kv in range(KV):
                qi = u * KV + kv
                if qi + 1 < NU * KV:
                    P.w("gpsimd", "mm", P.c("mm")) if qi >= 1 else None
                    P.do("gpsimd", lambda h, s, qi=qi: h.indirect_dma_start(
                        qslot(qi + 1), None, qpad_v, IndirectOffsetOnAxis(ap=IDX[:, qi + 1:qi + 2], axis=0)),
                        "qa" if (qi + 1) % 2 == 0 else "qb", 16)
                P.wl("tensor", "qa" if qi % 2 == 0 else "qb")
                for qh in range(4):
                    hq = kv * 4 + qh
                    ob = 4 + (step % 2)
                    for kt in range(4):
                        sb = 6 + (kt % 2)
                        P.w("tensor", "sc", exp_bank_sc[sb])
                        P.do("tensor", lambda h, s, kv=kv, qh=qh, kt=kt, sb=sb, qi=qi:
                             h.matmul(pb(sb), Kc(kv)[:, kt * 128:(kt + 1) * 128],
                                      qslot(qi)[:, qh * 512:(qh + 1) * 512],
                                      start=True, stop=True), "mm")
                        mm_s = P.c("mm")
                        lo = kt * 128 if diag else 0
                        P.w("scalar", "mm", max(mm_s, omm_of_p.get(kt, 0)))
                        P.do("scalar", lambda h, s, kt=kt, sb=sb, lo=lo:
                             h.activation(Pbuf(kt)[:, lo:512], pb(sb, lo, 512), AF.Exp), "sc")
                        exp_bank_sc[sb] = P.c("sc")
                        sc_exp = P.c("sc")
                        if diag:
                            P.w("vector", "sc", sc_exp)
                            P.do("vector", lambda h, s, kt=kt, lo=lo:
                                 h.tensor_mul(Pbuf(kt)[:, lo:lo + 128], Pbuf(kt)[:, lo:lo + 128], triS()), "ev")
                            P.w("tensor", "ev", P.c("ev"))
                        else:
                            P.w("tensor", "sc", sc_exp)
                        if kt == 0:
                            if len(stt_hist) >= 2:
                                P.w("tensor", "ev", stt_hist[-2])
                        P.do("tensor", lambda h, s, u=u, kv=kv, kt=kt, ob=ob:
                             h.matmul(pb(ob, 0, 512, 0, 65), Vc(u, kt)[:, kv * 65:(kv + 1) * 65],
                                      Pbuf(kt), start=(kt == 0), stop=(kt == 3)), "mm")
                        omm_of_p[kt] = P.c("mm")
                    mm_o = P.c("mm")
                    P.w("vector", "mm", mm_o)
                    P.do("vector", lambda h, s, u=u, hq=hq, ob=ob:
                         h.scalar_tensor_tensor(oaccA[:, hq * 512:(hq + 1) * 512], pb(ob, 0, 512, 0, 65),
                                                wcolS()[0:65, u:u + 1], oaccA[:, hq * 512:(hq + 1) * 512],
                                                ALU.mult, ALU.add), "ev")
                    P.do("vector", lambda h, s, u=u, hq=hq, ob=ob:
                         h.scalar_tensor_tensor(oaccB[:, hq * 512:(hq + 1) * 512], pb(ob, 0, 512, 0, 65),
                                                wcolS()[0:65, NU + u:NU + u + 1], oaccB[:, hq * 512:(hq + 1) * 512],
                                                ALU.mult, ALU.add), "ev")
                    stt_hist.append(P.c("ev"))
                    step += 1

        P.barrier()
        if debug:
            P.do("sync", lambda h, s: h.dma_start(out=d_oA, in_=oaccA), "dbg", 16)
            P.do("sync", lambda h, s: h.dma_start(out=d_oB, in_=oaccB), "dbg", 16)
            P.do("sync", lambda h, s: h.dma_start(out=d_ncr, in_=crS()), "dbg", 16)
        onT = AR1
        P.do("vector", lambda h, s: h.tensor_scalar_max(oaccA[64:65, :], oaccA[64:65, :], 1e-37), "ev")
        P.do("vector", lambda h, s: h.tensor_scalar_max(oaccB[64:65, :], oaccB[64:65, :], 1e-37), "ev")
        P.drain("vector")
        P.w("tensor", "ev", P.c("ev"))
        bc65S = lambda: A4[0:65, 902:966]
        lbr = lambda: A4[0:64, 966:1478]
        onscr = lambda: A4[0:64, 1478:1990]
        P.do("sync", lambda h, s: h.dma_start(out=bc65S(), in_=bc65), "boot2", 16)
        P.wl("tensor", "boot2")
        last_on_gq = 0
        for ch in range(2):
            oacc = oaccA if ch == 0 else oaccB
            for hq in range(NH):
                kvh = hq // 4
                fc = hq // 2
                hh = hq % 2
                b = hq % 2
                P.bank_w(b)
                P.do("tensor", lambda h, s, oacc=oacc, hq=hq, b=b:
                     h.matmul(pb(b, 0, 512, 0, 64), bc65S(), oacc[:, hq * 512:(hq + 1) * 512], start=True, stop=True), "mm")
                P.bank_wrote(b)
                P.bank_r("vector", b)
                P.w("vector", "mm", P.c("mm"))
                P.do("vector", lambda h, s, b=b: h.reciprocal(lbr(), pb(b, 0, 512, 0, 64)), "ev")
                P.bank_read("vector", b)
                P.wl("vector", "os")
                P.do("vector", lambda h, s, oacc=oacc, hq=hq:
                     h.tensor_mul(onscr(), oacc[0:64, hq * 512:(hq + 1) * 512], lbr()), "ev")
                P.do("vector", lambda h, s, kvh=kvh: h.tensor_scalar_add(onscr(), onscr(), bvS()[:, kvh:kvh + 1]), "ev")
                ev_on = P.c("ev")
                if hh == 0:
                    P.do("vector", lambda h, s, fc=fc, ch=ch: h.tensor_copy(
                        onT[0:64, fc * 1024 + ch * 512:fc * 1024 + ch * 512 + 512], onscr()), "ev")
                    last_on_ev = P.c("ev")
                else:
                    P.w("gpsimd", "ev", ev_on)
                    P.do("gpsimd", lambda h, s, fc=fc, ch=ch: h.dma_start(
                        out=onT[64:128, fc * 1024 + ch * 512:fc * 1024 + ch * 512 + 512], in_=onscr()), "os", 16)
        onT_ev = P.c("ev")

        P.barrier()
        wostg = AR3[:, 0:2048]
        woT = AR3[:, 4096:6144]
        xchunk = lambda i: AR3[:, 8192 + (i % 2) * 512:8192 + (i % 2) * 512 + 512]
        sqs2 = AR3[:, 9216:9728]
        sqs3 = AR3[:, 9728:10240]
        wo_ld = []
        P.wl("sync", "xsp")
        P.do("sync", lambda h, s: h.dma_start(out=wostg, in_=Wo[0:128, :]), "wq", 16)
        y1_hist = []
        for fo in range(16):
            P.wl("tensor", "wq")
            strip16(lambda j: wostg[:, j * 128:(j + 1) * 128],
                    lambda g: woT[:, g * 512:(g + 1) * 512])
            woT_done = P.c("ev")
            P.w("sync", "mm", P.c("mm"))
            if fo + 1 < 16:
                P.do("sync", lambda h, s, fo=fo: h.dma_start(out=wostg, in_=Wo[(fo + 1) * 128:(fo + 2) * 128, :]), "wq", 16)
            for ch in range(2):
                if len(y1_hist) >= 2:
                    P.w("sync", "ev", y1_hist[-2])
                P.do("sync", lambda h, s, fo=fo, ch=ch: h.dma_start(
                    out=xchunk(ch), in_=xTsp[:, fo * 1024 + ch * 512:fo * 1024 + ch * 512 + 512]), "xc" if ch == 0 else "xd", 16)
                b = ch
                P.w("tensor", "ev", woT_done)
                P.bank_w(b)
                for fi in range(16):
                    P.do("tensor", lambda h, s, fo=fo, fi=fi, ch=ch, b=b:
                         h.matmul(pb(b), woT[:, fi * 128:(fi + 1) * 128],
                                  onT[:, fi * 1024 + ch * 512:fi * 1024 + ch * 512 + 512],
                                  start=(fi == 0), stop=(fi == 15)), "mm")
                P.bank_wrote(b)
                P.bank_r("vector", b)
                P.wl("vector", "xc" if ch == 0 else "xd")
                P.do("vector", lambda h, s, fo=fo, ch=ch, b=b:
                     h.scalar_tensor_tensor(AR2[:, fo * 1024 + ch * 512:fo * 1024 + ch * 512 + 512],
                                            pb(b), boS()[:, fo:fo + 1], xchunk(ch),
                                            ALU.add, ALU.add), "ev")
                P.bank_read("vector", b)
                y1_hist.append(P.c("ev"))
        y1_done = P.c("ev")
        if debug:
            P.w("sync", "ev", y1_done)
            P.do("sync", lambda h, s: h.dma_start(out=d_onT, in_=onT[:]), "dbg", 16)
            P.do("sync", lambda h, s: h.dma_start(out=d_y1, in_=AR2[:]), "dbg", 16)
            P.wl("tensor", "dbg")
            P.wl("vector", "dbg")
            P.wl("scalar", "dbg")

        murow = lambda: A4[0:1, 1024:1536]
        isrow = lambda: A4[0:1, 1536:2048]
        r1row = lambda: A4[0:1, 2048:2560]
        P.w("tensor", "ev", y1_done)
        for ch in range(2):
            P.bank_w(2)
            for hc in range(HC):
                P.do("tensor", lambda h, s, hc=hc, ch=ch: h.matmul(pb(2, 0, 512, 0, 1), onescS(),
                                                                   AR2[:, hc * 1024 + ch * 512:hc * 1024 + ch * 512 + 512],
                                                                   start=(hc == 0), stop=(hc == HC - 1)), "mm")
            P.bank_wrote(2)
            last_sq_mm = 0
            P.bank_w(3)
            ev_now = P.c("ev")
            for hc in range(HC):
                P.w("scalar", "ev", ev_now)
                P.w("scalar", "mm", last_sq_mm)
                P.do("scalar", lambda h, s, hc=hc, ch=ch: h.activation(sqs2, AR2[:, hc * 1024 + ch * 512:hc * 1024 + ch * 512 + 512], AF.Square), "sc")
                sc_s = P.c("sc")
                P.w("tensor", "sc", sc_s)
                P.do("tensor", lambda h, s, hc=hc: h.matmul(pb(3, 0, 512, 0, 1), onescS(), sqs2,
                                                            start=(hc == 0), stop=(hc == HC - 1)), "mm")
                last_sq_mm = P.c("mm")
            P.bank_wrote(3)
            P.bank_r("vector", 2)
            P.bank_r("vector", 3)
            P.do("vector", lambda h, s: h.tensor_scalar_mul(murow(), pb(2, 0, 512, 0, 1), 1.0 / H), "ev")
            P.do("vector", lambda h, s: h.tensor_mul(r1row(), murow(), murow()), "ev")
            P.do("vector", lambda h, s: h.scalar_tensor_tensor(isrow(), pb(3, 0, 512, 0, 1), 1.0 / H, r1row(), ALU.mult, ALU.subtract), "ev")
            P.bank_read("vector", 2)
            P.bank_read("vector", 3)
            P.do("vector", lambda h, s: h.tensor_scalar_add(isrow(), isrow(), 1e-5), "ev")
            P.w("scalar", "ev", P.c("ev"))
            P.do("scalar", lambda h, s: h.activation(isrow(), isrow(), AF.Sqrt), "sc")
            P.w("vector", "sc", P.c("sc"))
            P.do("vector", lambda h, s: h.reciprocal(isrow(), isrow()), "ev")
            P.w("tensor", "ev", P.c("ev"))
            P.bank_w(2); P.bank_w(3)
            P.do("tensor", lambda h, s: h.matmul(pb(2), onesrS(), murow(), start=True, stop=True), "mm")
            P.do("tensor", lambda h, s: h.matmul(pb(3), onesrS(), isrow(), start=True, stop=True), "mm")
            P.bank_wrote(2); P.bank_wrote(3)
            P.bank_r("vector", 2); P.bank_r("vector", 3)
            for hc in range(HC):
                P.do("vector", lambda h, s, hc=hc, ch=ch: h.tensor_sub(sqs3, AR2[:, hc * 1024 + ch * 512:hc * 1024 + ch * 512 + 512], pb(2)), "ev")
                P.do("vector", lambda h, s, hc=hc, ch=ch: h.tensor_mul(AR2[:, hc * 1024 + ch * 512:hc * 1024 + ch * 512 + 512], sqs3, pb(3)), "ev")
            P.bank_read("vector", 2); P.bank_read("vector", 3)
        x1_done = P.c("ev")
        if debug:
            P.w("sync", "ev", x1_done)
            P.do("sync", lambda h, s: h.dma_start(out=d_x1, in_=AR2[:]), "dbg", 16)

        P.barrier()
        gsS = lambda: A4[:, 1024:1536]
        hbS = lambda: A4[:, 1536:2048]
        amS = lambda: A4[:, 2048:2560]
        geS = lambda i: A4[:, 2560 + i * 128:2560 + (i + 1) * 128]
        cs0 = lambda: A4[:, 3328:3456]
        stat = lambda i: A4[:, 3456 + i:3457 + i]
        hsT = lambda: A4[:, 2048:2560]
        out2 = AR1[:, 8192:16384]
        Wdb = AR1[:, 0:8192]
        Wgb = AR3[:, 0:8192]
        Wub = AR3[:, 8192:16384]

        for half in range(2):
            ch = half
            P.w("vector", "mm", P.c("mm"))
            P.wl("vector", "os", "st")
            P.do("vector", lambda h, s: h.memset(out2, 0.0), "ev")
            out2_init = P.c("ev")
            ld_g = {}
            def load_ib(ib):
                P.w("sync", "mm", P.c("mm"))
                P.w("sync", "ev", P.c("ev"))
                for hc2 in range(HC):
                    P.do("sync", lambda h, s, ib=ib, hc2=hc2: h.dma_start(
                        out=Wgb[:, hc2 * 512:(hc2 + 1) * 512],
                        in_=Wg[hc2 * 128:(hc2 + 1) * 128, ib * 512:(ib + 1) * 512]), "fw", 16)
                    P.do("sync", lambda h, s, ib=ib, hc2=hc2: h.dma_start(
                        out=Wub[:, hc2 * 512:(hc2 + 1) * 512],
                        in_=Wu[hc2 * 128:(hc2 + 1) * 128, ib * 512:(ib + 1) * 512]), "fw", 16)
                for ic in range(4):
                    P.do("sync", lambda h, s, ib=ib, ic=ic: h.dma_start(
                        out=Wdb[:, ic * 2048:(ic + 1) * 2048],
                        in_=Wd[ib * 512 + ic * 128:ib * 512 + (ic + 1) * 128, :]), "fw", 16)
                ld_g[ib] = P.c("fw")
            load_ib(0)
            for ib in range(16):
                P.w("tensor", "fw", ld_g[ib])
                P.w("tensor", "ev", out2_init)
                for tt in range(4):
                    x1s = lambda hc2, tt=tt, ch=ch: AR2[:, hc2 * 1024 + ch * 512 + tt * 128:hc2 * 1024 + ch * 512 + tt * 128 + 128]
                    P.bank_w(0)
                    for hc2 in range(HC):
                        P.do("tensor", lambda h, s, hc2=hc2, x1s=x1s:
                             h.matmul(pb(0), x1s(hc2), Wgb[:, hc2 * 512:(hc2 + 1) * 512],
                                      start=(hc2 == 0), stop=(hc2 == HC - 1)), "mm")
                    P.bank_wrote(0)
                    mm_g = P.c("mm")
                    P.bank_w(1)
                    for hc2 in range(HC):
                        P.do("tensor", lambda h, s, hc2=hc2, x1s=x1s:
                             h.matmul(pb(1), x1s(hc2), Wub[:, hc2 * 512:(hc2 + 1) * 512],
                                      start=(hc2 == 0), stop=(hc2 == HC - 1)), "mm")
                    P.bank_wrote(1)
                    mm_u = P.c("mm")
                    P.bank_r("scalar", 0)
                    P.w("scalar", "ev", P.c("ev"))
                    P.do("scalar", lambda h, s: h.activation(gsS(), pb(0), AF.Silu), "sc")
                    P.bank_read("scalar", 0)
                    sc_g = P.c("sc")
                    P.bank_r("vector", 1)
                    P.w("vector", "sc", sc_g)
                    P.do("vector", lambda h, s: h.tensor_mul(hbS(), gsS(), pb(1)), "ev")
                    P.bank_read("vector", 1)
                    ev_h = P.c("ev")
                    P.w("scalar", "ev", ev_h)
                    P.do("scalar", lambda h, s: h.activation(amS().rearrange("p (f g) -> p g f", f=4), hbS(), AF.Abs), "sc")
                    sc_a = P.c("sc")
                    P.w("vector", "sc", sc_a)
                    Am = lambda i: amS()[:, i * 128:(i + 1) * 128]
                    hb3 = hbS().rearrange("p (g f) -> p g f", f=4)
                    hv = lambda i: hb3[:, :, i:i + 1]
                    pairs = [(0, 1), (0, 2), (0, 3), (1, 2), (1, 3), (2, 3)]
                    for pi_, (a, b_) in enumerate(pairs):
                        P.do("vector", lambda h, s, pi_=pi_, a=a, b_=b_: h.tensor_tensor(geS(pi_), Am(a), Am(b_), ALU.is_ge), "ev")
                    P.do("vector", lambda h, s: h.tensor_add(cs0(), geS(0), geS(1)), "ev")
                    P.do("vector", lambda h, s: h.tensor_add(cs0(), cs0(), geS(2)), "ev")
                    P.do("vector", lambda h, s: h.scalar_tensor_tensor(hv(0), cs0().rearrange("p (g o) -> p g o", o=1), 2.0, hv(0), ALU.is_ge, ALU.mult), "ev")
                    P.do("vector", lambda h, s: h.tensor_add(cs0(), geS(3), geS(4)), "ev")
                    P.do("vector", lambda h, s: h.tensor_sub(cs0(), cs0(), geS(0)), "ev")
                    P.do("vector", lambda h, s: h.scalar_tensor_tensor(hv(1), cs0().rearrange("p (g o) -> p g o", o=1), 1.0, hv(1), ALU.is_ge, ALU.mult), "ev")
                    P.do("vector", lambda h, s: h.tensor_sub(cs0(), geS(5), geS(1)), "ev")
                    P.do("vector", lambda h, s: h.tensor_sub(cs0(), cs0(), geS(3)), "ev")
                    P.do("vector", lambda h, s: h.scalar_tensor_tensor(hv(2), cs0().rearrange("p (g o) -> p g o", o=1), 0.0, hv(2), ALU.is_ge, ALU.mult), "ev")
                    P.do("vector", lambda h, s: h.tensor_add(cs0(), geS(2), geS(4)), "ev")
                    P.do("vector", lambda h, s: h.tensor_add(cs0(), cs0(), geS(5)), "ev")
                    P.do("vector", lambda h, s: h.scalar_tensor_tensor(hv(3), cs0().rearrange("p (g o) -> p g o", o=1), 1.0, hv(3), ALU.is_le, ALU.mult), "ev")
                    ev_mask = P.c("ev")
                    if debug and half == 0 and ib == 0 and tt == 0:
                        P.w("sync", "ev", ev_mask)
                        P.do("sync", lambda h, s: h.dma_start(out=d_h0, in_=hbS()), "dbg", 16)
                        P.wl("vector", "dbg")
                        P.wl("scalar", "dbg")
                    P.w("tensor", "ev", ev_mask)
                    P.bank_w(2)
                    for ic in range(4):
                        P.do("tensor", lambda h, s, ic=ic: h.transpose(pb(2, ic * 128, (ic + 1) * 128), hbS()[:, ic * 128:(ic + 1) * 128], identS()), "mm")
                    P.bank_wrote(2)
                    P.bank_r("vector", 2)
                    P.do("vector", lambda h, s: h.tensor_copy(hsT(), pb(2)), "ev")
                    P.bank_read("vector", 2)
                    ev_hsT = P.c("ev")
                    P.w("tensor", "ev", ev_hsT)
                    for fb in range(4):
                        P.bank_w(3)
                        for ic in range(4):
                            P.do("tensor", lambda h, s, ic=ic, fb=fb:
                                 h.matmul(pb(3), hsT()[:, ic * 128:(ic + 1) * 128],
                                          Wdb[:, ic * 2048 + fb * 512:ic * 2048 + (fb + 1) * 512],
                                          start=(ic == 0), stop=(ic == 3)), "mm")
                        P.bank_wrote(3)
                        P.bank_r("vector", 3)
                        P.do("vector", lambda h, s, tt=tt, fb=fb:
                             h.tensor_add(out2[:, tt * 2048 + fb * 512:tt * 2048 + (fb + 1) * 512],
                                          out2[:, tt * 2048 + fb * 512:tt * 2048 + (fb + 1) * 512],
                                          pb(3)), "ev")
                        P.bank_read("vector", 3)
                if ib + 1 < 16:
                    load_ib(ib + 1)
            for tt in range(4):
                for hc2 in range(HC):
                    b = 4 + hc2 % 2
                    P.bank_w(b)
                    P.do("tensor", lambda h, s, hc2=hc2, tt=tt, ch=ch, b=b:
                         h.transpose(pb(b, 0, 128), AR2[:, hc2 * 1024 + ch * 512 + tt * 128:hc2 * 1024 + ch * 512 + tt * 128 + 128], identS()), "mm")
                    P.bank_wrote(b)
                    P.bank_r("vector", b)
                    P.do("vector", lambda h, s, hc2=hc2, tt=tt, b=b:
                         h.tensor_add(out2[:, tt * 2048 + hc2 * 128:tt * 2048 + (hc2 + 1) * 128],
                                      out2[:, tt * 2048 + hc2 * 128:tt * 2048 + (hc2 + 1) * 128],
                                      pb(b, 0, 128)), "ev")
                    P.bank_read("vector", b)
                P.do("vector", lambda h, s, tt=tt: h.reduce_sum(stat(0), out2[:, tt * 2048:(tt + 1) * 2048], axis=AX.X), "ev")
                ev_sum = P.c("ev")
                P.w("scalar", "ev", ev_sum)
                for q4 in range(4):
                    P.do("scalar", lambda h, s, tt=tt, q4=q4: h.activation(
                        amS(), out2[:, tt * 2048 + q4 * 512:tt * 2048 + (q4 + 1) * 512], AF.Square,
                        accum_out=stat(1 + q4)), "sc")
                sc_ss = P.c("sc")
                P.w("vector", "sc", sc_ss)
                P.do("vector", lambda h, s: h.reduce_sum(stat(5), A4[:, 3457:3461], axis=AX.X), "ev")
                P.drain("vector")
                P.do("vector", lambda h, s: h.tensor_scalar_mul(stat(6), stat(0), 1.0 / H), "ev")
                P.drain("vector")
                P.do("vector", lambda h, s: h.tensor_mul(stat(7), stat(6), stat(6)), "ev")
                P.drain("vector")
                P.do("vector", lambda h, s: h.scalar_tensor_tensor(stat(8), stat(5), 1.0 / H, stat(7), ALU.mult, ALU.subtract), "ev")
                P.drain("vector")
                P.do("vector", lambda h, s: h.tensor_scalar_add(stat(8), stat(8), 1e-5), "ev")
                P.w("scalar", "ev", P.c("ev"))
                P.do("scalar", lambda h, s: h.activation(stat(9), stat(8), AF.Sqrt), "sc")
                P.w("vector", "sc", P.c("sc"))
                P.do("vector", lambda h, s: h.reciprocal(stat(10), stat(9)), "ev")
                P.drain("vector")
                P.do("vector", lambda h, s, tt=tt: h.tensor_scalar(out2[:, tt * 2048:(tt + 1) * 2048],
                                                                   out2[:, tt * 2048:(tt + 1) * 2048],
                                                                   stat(6), stat(10),
                                                                   ALU.subtract, ALU.mult), "ev")
                ev_out = P.c("ev")
                P.w("sync", "ev", ev_out)
                P.do("sync", lambda h, s, tt=tt, half=half: h.dma_start(
                    out=y[half * 512 + tt * 128:half * 512 + (tt + 1) * 128, :],
                    in_=out2[:, tt * 2048:(tt + 1) * 2048]), "st", 16)

        for ln_ in ("st", "q2s", "qsp", "sp", "cw", "os", "xsp", "dbg"):
            P.wl("sync", ln_)

        @block.sync
        def _(h):
            for fn in P.ops["sync"]:
                fn(h, sems)

        @block.tensor
        def _(h):
            for fn in P.ops["tensor"]:
                fn(h, sems)

        @block.vector
        def _(h):
            for fn in P.ops["vector"]:
                fn(h, sems)

        @block.scalar
        def _(h):
            for fn in P.ops["scalar"]:
                fn(h, sems)

        @block.gpsimd
        def _(h):
            for fn in P.ops["gpsimd"]:
                fn(h, sems)

    return nc


def host_inputs(core, inp):
    c = core
    cA, cB = c, 7 - c
    x = np.asarray(inp["L0_attn_x"], dtype=np.float32)
    b0 = [x[0, k * 512:(k + 1) * 512] for k in range(8)]
    b1 = [x[1, k * 512:(k + 1) * 512] for k in range(8)]
    x_own = np.concatenate([b0[cA], b1[cB]], axis=0)
    pre = [b0[k] for k in range(cA)] + [b1[k] for k in range(cB)] + [b0[cA], b1[cB]]
    xkeys = np.concatenate(pre, axis=0)
    unit_pos = list(range(cA)) + list(range(cB)) + [cA, cB]
    unit_sel = [0] * cA + [1] * cB + [0, 1]

    wselr = np.zeros((1, 18), np.float32)
    for u in range(9):
        wselr[0, u] = 1.0 if unit_sel[u] == 0 else 0.0
        wselr[0, 9 + u] = 1.0 - wselr[0, u]

    p_ = np.arange(128, dtype=np.uint32)
    qgidx = np.zeros((128, 72), np.uint32)
    for u in range(9):
        for kv in range(8):
            qgidx[:, u * 8 + kv] = unit_sel[u] * 1024 + kv * 128 + p_
    tabidx = np.zeros((128, 11), np.uint32)
    for u in range(9):
        tabidx[:, u] = p_ * 8 + unit_pos[u]
    tabidx[:, 9] = p_ * 8 + cA
    tabidx[:, 10] = p_ * 8 + cB

    invf = 10000.0 ** (-(np.arange(32, dtype=np.float64) * 2) / 64.0)
    t64 = np.arange(4096, dtype=np.float64)
    ang = invf[:, None] * t64[None, :]
    nsin32 = np.round(ang / (2 * np.pi))
    ncos32 = np.round((ang + np.pi / 2) / (2 * np.pi))
    nsin = np.concatenate([nsin32, nsin32], 0).astype(np.float32)
    ncos = np.concatenate([ncos32, ncos32], 0).astype(np.float32)
    tb = np.tile(np.arange(4096, dtype=np.float32)[None, :], (64, 1))
    invf64_2 = np.concatenate([invf, invf])
    invfh = invf64_2.astype(np.float32)
    invfl = (invf64_2 - invfh.astype(np.float64)).astype(np.float32)
    invfh = invfh.reshape(64, 1); invfl = invfl.reshape(64, 1)

    ident = np.eye(128, dtype=np.float32)
    Pi = np.zeros((128, 128), np.float32)
    for i in range(128):
        g = (i // 64) * 64
        Pi[i, g + (i - g + 32) % 64] = 1.0
    E2 = np.zeros((2, 128), np.float32)
    E2[0, 0:64] = SCALE
    E2[1, 64:128] = SCALE
    onesr = np.ones((1, 128), np.float32)
    onesc = np.ones((128, 1), np.float32)
    ones2 = np.zeros((128, 2), np.float32)
    ones2[0:64, 0] = 1.0
    ones2[64:128, 1] = 1.0
    tri = (np.arange(128)[:, None] <= np.arange(128)[None, :]).astype(np.float32)
    bc65 = np.zeros((65, 64), np.float32)
    bc65[64, :] = 1.0

    return {
        "x_own": np.ascontiguousarray(x_own),
        "xkeys": np.ascontiguousarray(xkeys),
        "Wq": np.asarray(inp["L0_attn_Wq"], np.float32),
        "Wk": np.asarray(inp["L0_attn_Wk"], np.float32),
        "Wv": np.asarray(inp["L0_attn_Wv"], np.float32),
        "Wo": np.asarray(inp["L0_attn_Wo"], np.float32),
        "Wg": np.asarray(inp["L0_ffn_w_gate"], np.float32),
        "Wu": np.asarray(inp["L0_ffn_w_up"], np.float32),
        "Wd": np.asarray(inp["L0_ffn_w_down"], np.float32),
        "bqc": np.ascontiguousarray(np.asarray(inp["L0_attn_bq"], np.float32).reshape(16, 128).T),
        "bkc": np.ascontiguousarray(np.asarray(inp["L0_attn_bk"], np.float32).reshape(4, 128).T),
        "bvc": np.ascontiguousarray(np.asarray(inp["L0_attn_bv"], np.float32).reshape(8, 64).T),
        "boc": np.ascontiguousarray(np.asarray(inp["L0_attn_bo"], np.float32).reshape(16, 128).T),
        "kwpair": np.ascontiguousarray(np.asarray(inp["L0_attn_key_weights"], np.float32).reshape(16, 2).T),
        "ident": ident, "Pi": Pi, "E2": E2, "onesr": onesr, "onesc": onesc, "ones2": ones2,
        "tri": tri, "bc65": bc65, "invfh": invfh, "invfl": invfl, "tb": tb, "nsin": nsin, "ncos": ncos,
        "wselr": wselr, "qgidx": qgidx, "tabidx": tabidx,
    }


def kernel(**inp):
    nc = build_program(debug=False)
    in_maps = [host_inputs(c, inp) for c in range(8)]
    res = run_bass_kernel_spmd(nc, in_maps, core_ids=list(range(8)))
    out = np.empty((2, 4096, 2048), np.float32)
    for c in range(8):
        yv = np.asarray(res.results[c]["y"])
        out[0, c * 512:(c + 1) * 512] = yv[0:512]
        out[1, (7 - c) * 512:(8 - c) * 512] = yv[512:1024]
    return out


# version 04
# speedup vs baseline: 1.3161x; 1.3161x over previous
import numpy as np
import concourse.bass as bass
import concourse.mybir as mybir
from concourse.bass_utils import run_bass_kernel_spmd
from concourse.bass_core_rust import IndirectOffsetOnAxis

F32 = mybir.dt.float32
U32 = mybir.dt.uint32
AF = mybir.ActivationFunctionType
ALU = mybir.AluOpType
AX = mybir.AxisListType

H = 2048
HC = 16
DH = 64
NH = 32
KV = 8
BLK = 512
NU = 9
FF = 8192
SCALE = DH ** -0.5

TWO_PI = 2.0 * np.pi
C1 = float(np.float32(TWO_PI)); C2 = float(np.float32(TWO_PI - np.float64(np.float32(TWO_PI))))
LN1E4 = np.log(10000.0)
L1 = float(np.float32(LN1E4)); L2 = float(np.float32(LN1E4 - np.float64(np.float32(LN1E4))))
HP1 = float(np.float32(np.pi / 2)); HP2 = float(np.float32(np.pi / 2 - np.float64(np.float32(np.pi / 2))))

ENGINES = ("sync", "tensor", "vector", "scalar", "gpsimd")
ESEM = {"vector": "ev", "scalar": "sc", "tensor": "mm", "gpsimd": "gc", "sync": "sq"}


class Prog:
    def __init__(self):
        self.ops = {e: [] for e in ENGINES}
        self.cnt = {}
        self.bwr = {}
        self.brd = {}

    def w(self, eng, sem, val):
        if val and val > 0:
            self.ops[eng].append(lambda h, sems, s=sem, v=val: h.wait_ge(sems[s], v))

    def do(self, eng, build, sem=None, amt=1):
        if eng in ("vector", "scalar") and sem in ("ev", "sc"):
            self.ops[eng].append(lambda h, sems: h.drain())
        if sem is None:
            self.ops[eng].append(lambda h, sems, b=build: b(h, sems))
            return None
        self.cnt[sem] = self.cnt.get(sem, 0) + amt
        def fn(h, sems, b=build, s=sem, a=amt):
            b(h, sems).then_inc(sems[s], a)
        self.ops[eng].append(fn)
        return self.cnt[sem]

    def c(self, sem):
        return self.cnt.get(sem, 0)

    def wl(self, eng, *lanes):
        for ln in lanes:
            self.w(eng, ln, self.c(ln))

    def drain(self, eng):
        self.ops[eng].append(lambda h, sems: h.drain())

    def barrier(self):
        snap = dict(self.cnt)
        for e in ENGINES:
            for s, v in snap.items():
                self.w(e, s, v)
        self.bwr = {}; self.brd = {}

    def bank_w(self, b):
        g = self.brd.get(b)
        if g:
            self.w("tensor", g[0], g[1])

    def bank_wrote(self, b):
        self.bwr[b] = ("mm", self.c("mm"))

    def bank_r(self, eng, b):
        g = self.bwr.get(b)
        if g:
            self.w(eng, g[0], g[1])

    def bank_read(self, eng, b):
        self.brd[b] = (ESEM[eng], self.c(ESEM[eng]))


def build_program(debug=False):
    nc = bass.Bass("TRN2", target_bir_lowering=False)
    d = {}
    def din(name, shape, dt=F32):
        d[name] = nc.dram_tensor(name, list(shape), dt, kind="ExternalInput").ap()
        return d[name]

    x_own = din("x_own", [1024, H])
    xkeys = din("xkeys", [NU * BLK, H])
    Wq = din("Wq", [H, H]); Wk = din("Wk", [512, H]); Wv = din("Wv", [512, H]); Wo = din("Wo", [H, H])
    Wg = din("Wg", [H, FF]); Wu = din("Wu", [H, FF]); Wd = din("Wd", [FF, H])
    bqc = din("bqc", [128, 16]); bkc = din("bkc", [128, 4]); bvc = din("bvc", [64, 8]); boc = din("boc", [128, 16])
    kwpair = din("kwpair", [2, 16])
    ident = din("ident", [128, 128]); Pi = din("Pi", [128, 128]); E2 = din("E2", [2, 128])
    onesr = din("onesr", [1, 128]); onesc = din("onesc", [128, 1]); ones2 = din("ones2", [128, 2])
    bc65 = din("bc65", [65, 64])
    tri = din("tri", [128, 128])
    invfh = din("invfh", [64, 1]); invfl = din("invfl", [64, 1]); tb = din("tb", [64, 4096])
    nsin = din("nsin", [64, 4096]); ncos = din("ncos", [64, 4096])
    wselr = din("wselr", [1, 2 * NU])
    qgidx = din("qgidx", [128, NU * KV], U32)
    tabidx = din("tabidx", [128, NU + 2], U32)

    y = nc.dram_tensor("y", [1024, H], F32, kind="ExternalOutput").ap()

    def dscr(name, shape):
        if debug:
            return nc.dram_tensor(name, list(shape), F32, kind="ExternalOutput").ap()
        return nc.dram_tensor(name, list(shape), F32).ap()

    qpad = dscr("qpad", [2 * KV * 128, 4 * BLK])
    ksp = dscr("ksp", [NU, 8 * 128, BLK])
    q2d = dscr("q2d", [2, 16, 2, BLK])
    vsp = dscr("vsp", [NU, 4 * 128, 520])
    cosd = dscr("cosd", [128, 4096])
    sind = dscr("sind", [128, 4096])
    xTsp = dscr("xTsp", [128, HC * 1024])
    if debug:
        d_oA = nc.dram_tensor("d_oA", [65, 16384], F32, kind="ExternalOutput").ap()
        d_oB = nc.dram_tensor("d_oB", [65, 16384], F32, kind="ExternalOutput").ap()
        d_ncr = nc.dram_tensor("d_ncr", [1, 32], F32, kind="ExternalOutput").ap()
        d_mq = nc.dram_tensor("d_mq", [1, 32], F32, kind="ExternalOutput").ap()
        d_mk = nc.dram_tensor("d_mk", [1, 8], F32, kind="ExternalOutput").ap()
        d_onT = nc.dram_tensor("d_onT", [128, 16384], F32, kind="ExternalOutput").ap()
        d_y1 = nc.dram_tensor("d_y1", [128, 16384], F32, kind="ExternalOutput").ap()
        d_x1 = nc.dram_tensor("d_x1", [128, 16384], F32, kind="ExternalOutput").ap()
        d_h0 = nc.dram_tensor("d_h0", [128, 512], F32, kind="ExternalOutput").ap()
        d_qt = nc.dram_tensor("d_qt", [128, 1536], F32, kind="ExternalOutput").ap()
        d_cr = nc.dram_tensor("d_cr", [1, 8192], F32, kind="ExternalOutput").ap()
        d_kk = nc.dram_tensor("d_kk", [128, 3072], F32, kind="ExternalOutput").ap()
        d_kn = nc.dram_tensor("d_kn", [NU, 4, 2, 512], F32, kind="ExternalOutput").ap()
        d_o2 = nc.dram_tensor("d_o2", [128, 8192], F32, kind="ExternalOutput").ap()

    P = Prog()

    import contextlib
    _es = contextlib.ExitStack()
    with _es:
        AR1 = _es.enter_context(nc.sbuf_tensor("AR1", [128, 16384], F32))
        AR2 = _es.enter_context(nc.sbuf_tensor("AR2", [128, 16384], F32))
        AR3 = _es.enter_context(nc.sbuf_tensor("AR3", [128, 16384], F32))
        AR4 = _es.enter_context(nc.sbuf_tensor("AR4", [128, 3472], F32))
        IDX = _es.enter_context(nc.sbuf_tensor("IDX", [128, 96], U32))
        PS = _es.enter_context(nc.psum_tensor("PS", [128, 4096], F32))
        _lanes = ["mm", "ev", "sc", "boot", "boot2", "tsp", "qtg", "qz", "xa", "xb", "wq",
                  "q2s", "qsp", "utg", "sp", "pi", "crl", "cw", "ks", "va", "vb", "qa", "qb",
                  "os", "xc", "xd", "xsp", "fw", "fd", "st", "dbg"]
        _semh = {n: _es.enter_context(nc.semaphore("s_" + n)) for n in _lanes}
        block = _es.enter_context(nc.Block())
        sems = _semh

        def pb(i, lo=0, hi=512, p0=0, p1=128):
            return PS[p0:p1, i * 512 + lo:i * 512 + hi]

        A4 = AR4
        identS = lambda: A4[:, 0:128]
        PiS = lambda: A4[:, 128:256]
        triS = lambda: A4[:, 256:384]
        onesrS = lambda: A4[0:1, 384:512]
        onescS = lambda: A4[:, 512:513]
        ones2S = lambda: A4[:, 514:516]
        E2S = lambda: A4[0:2, 516:644]
        kwpS = lambda: A4[0:2, 644:660]
        bqS = lambda: A4[:, 660:676]
        bkS = lambda: A4[:, 676:680]
        bvS = lambda: A4[0:64, 680:688]
        boS = lambda: A4[:, 688:704]
        kwcolS = lambda: A4[:, 704:720]
        ccolS = lambda: A4[:, 720:752]
        wcolS = lambda: A4[:, 752:770]
        Mq2S = lambda: A4[0:1, 770:802]
        Mk2S = lambda: A4[0:1, 802:810]
        tmp2S = lambda: A4[0:2, 810:812]
        c2S = lambda: A4[0:1, 814:846]
        crS = lambda: A4[0:1, 846:878]
        invfhS = lambda: A4[0:64, 878:879]
        invflS = lambda: A4[0:64, 879:880]
        wsS = lambda: A4[0:1, 884:902]
        qcosS = lambda ch: A4[:, 1024 + ch * 1024:1024 + ch * 1024 + 512]
        qsinS = lambda ch: A4[:, 1536 + ch * 1024:1536 + ch * 1024 + 512]
        q2stgS = lambda: AR3[0:2, 8192:8704]
        ucosS = lambda: A4[:, 1024:1536]
        usinS = lambda: A4[:, 1536:2048]

        for (dst, src) in [(identS(), ident), (PiS(), Pi), (triS(), tri),
                           (onesrS(), onesr), (onescS(), onesc), (ones2S(), ones2),
                           (E2S(), E2), (kwpS(), kwpair), (bqS(), bqc), (bkS(), bkc),
                           (bvS(), bvc), (boS(), boc), (wsS(), wselr),
                           (invfhS(), invfh), (invflS(), invfl)]:
            P.do("sync", lambda h, s, a=dst, b=src: h.dma_start(out=a, in_=b), "boot", 16)
        P.do("sync", lambda h, s: h.dma_start(out=IDX[:, 0:NU * KV], in_=qgidx), "boot", 16)
        P.do("sync", lambda h, s: h.dma_start(out=IDX[:, 72:72 + NU + 2], in_=tabidx), "boot", 16)
        tbS = AR2[0:64, 0:4096]
        nsS = AR2[0:64, 4096:8192]
        ncS = AR2[0:64, 8192:12288]
        angS = AR2[0:64, 12288:16384]
        redsS = AR3[0:64, 0:4096]
        redcS = AR3[0:64, 4096:8192]
        cosS = AR3[0:64, 8192:12288]
        sinS_ = AR3[0:64, 12288:16384]
        P.do("sync", lambda h, s: h.dma_start(out=tbS, in_=tb), "boot", 16)
        P.do("sync", lambda h, s: h.dma_start(out=nsS, in_=nsin), "boot", 16)
        P.do("sync", lambda h, s: h.dma_start(out=ncS, in_=ncos), "boot", 16)
        for e in ("vector", "scalar", "tensor"):
            P.wl(e, "boot")
        P.do("vector", lambda h, s: h.tensor_scalar_mul(angS, tbS, invfhS()), "ev")
        P.do("vector", lambda h, s: h.scalar_tensor_tensor(angS, tbS, invflS(), angS, ALU.mult, ALU.add), "ev")
        P.do("vector", lambda h, s: h.scalar_tensor_tensor(redsS, nsS, -C1, angS, ALU.mult, ALU.add), "ev")
        P.do("vector", lambda h, s: h.scalar_tensor_tensor(redsS, nsS, -C2, redsS, ALU.mult, ALU.add), "ev")
        P.do("vector", lambda h, s: h.scalar_tensor_tensor(redcS, ncS, -C1, angS, ALU.mult, ALU.add), "ev")
        P.do("vector", lambda h, s: h.scalar_tensor_tensor(redcS, ncS, -C2, redcS, ALU.mult, ALU.add), "ev")
        P.do("vector", lambda h, s: h.tensor_scalar_add(redcS, redcS, HP1), "ev")
        P.do("vector", lambda h, s: h.tensor_scalar_add(redcS, redcS, HP2), "ev")
        P.w("scalar", "ev", P.c("ev"))
        P.do("scalar", lambda h, s: h.activation(cosS, redcS, AF.Sin), "sc")
        P.do("scalar", lambda h, s: h.activation(sinS_[0:32, :], redsS[0:32, :], AF.Sin, scale=-1.0), "sc")
        P.do("scalar", lambda h, s: h.activation(sinS_[32:64, :], redsS[32:64, :], AF.Sin), "sc")
        s_tab = P.c("sc")
        P.w("sync", "sc", s_tab)
        P.do("sync", lambda h, s: h.dma_start(out=cosd[0:64, :], in_=cosS), "tsp", 16)
        P.do("sync", lambda h, s: h.dma_start(out=cosd[64:128, :], in_=cosS), "tsp", 16)
        P.do("sync", lambda h, s: h.dma_start(out=sind[0:64, :], in_=sinS_), "tsp", 16)
        P.do("sync", lambda h, s: h.dma_start(out=sind[64:128, :], in_=sinS_), "tsp", 16)

        P.bank_w(0)
        P.do("tensor", lambda h, s: h.matmul(PS[0:128, 0:16], E2S(), kwpS(), start=True, stop=True), "mm")
        P.do("tensor", lambda h, s: h.matmul(PS[0:128, 16:34], onesrS(), wsS(), start=True, stop=True), "mm")
        P.bank_wrote(0)
        P.bank_r("vector", 0)
        P.do("vector", lambda h, s: h.tensor_copy(kwcolS(), PS[0:128, 0:16]), "ev")
        P.do("vector", lambda h, s: h.tensor_copy(wcolS(), PS[0:128, 16:34]), "ev")
        P.bank_read("vector", 0)
        P.do("vector", lambda h, s: h.memset(Mk2S(), 0.0), "ev")

        cosd_v = cosd[:].rearrange("p (a b) -> (p a) b", b=512)
        sind_v = sind[:].rearrange("p (a b) -> (p a) b", b=512)
        P.wl("gpsimd", "tsp", "boot")
        for ch in range(2):
            P.do("gpsimd", lambda h, s, ch=ch: h.indirect_dma_start(
                qcosS(ch), None, cosd_v, IndirectOffsetOnAxis(ap=IDX[:, 72 + NU + ch:73 + NU + ch], axis=0)), "qtg", 16)
            P.do("gpsimd", lambda h, s, ch=ch: h.indirect_dma_start(
                qsinS(ch), None, sind_v, IndirectOffsetOnAxis(ap=IDX[:, 72 + NU + ch:73 + NU + ch], axis=0)), "qtg", 16)
        qtab_done = None

        zb = AR1[:, 0:2048]
        P.do("vector", lambda h, s: h.memset(zb, 0.0), "ev")
        P.w("gpsimd", "ev", P.c("ev"))
        for blk_ in range(16):
            P.do("gpsimd", lambda h, s, b=blk_: h.dma_start(out=qpad[b * 128:(b + 1) * 128, :], in_=zb), "qz", 16)
        qpad_zeroed = None

        def strip16(src_fn, dst_fn, banks=(4, 5), evac_eng="vector"):
            for g in range(4):
                b = banks[g % 2]
                P.bank_w(b)
                for j in range(4):
                    P.do("tensor", lambda h, s, src=src_fn(g * 4 + j), b=b, j=j:
                         h.transpose(pb(b, j * 128, (j + 1) * 128), src, identS()), "mm")
                P.bank_wrote(b)
                P.bank_r(evac_eng, b)
                P.do(evac_eng, lambda h, s, dst=dst_fn(g), b=b:
                     (h.tensor_copy(dst, pb(b)) if evac_eng == "vector" else h.copy(dst, pb(b))),
                     ESEM[evac_eng])
                P.bank_read(evac_eng, b)

        xstg = lambda i: AR3[:, (i % 2) * 2048:(i % 2) * 2048 + 2048]
        wraw = AR3[:, 4096:6144]
        wTb = AR3[:, 6144:8192]
        qsS = AR3[:, 10240:10752]
        m1S = AR3[:, 10752:11264]
        m2S = AR3[:, 11264:11776]
        qtS = AR3[:, 11776:12288]
        sqS = AR3[:, 12288:12800]

        P.w("sync", "sc", s_tab)
        P.wl("scalar", "tsp")
        P.wl("vector", "qz")
        xT3 = AR1[:].rearrange("p (hc t) -> p hc t", t=1024)
        x_ld = []
        for tt in range(8):
            P.w("sync", "mm", P.c("mm")) if tt >= 2 else None
            xl_ = "xa" if tt % 2 == 0 else "xb"
            P.do("sync", lambda h, s, tt=tt: h.dma_start(out=xstg(tt), in_=x_own[tt * 128:(tt + 1) * 128, :]), xl_, 16)
            P.wl("tensor", xl_)
            strip16(lambda j, tt=tt: xstg(tt)[:, j * 128:(j + 1) * 128],
                    lambda g, tt=tt: xT3[:, g * 4:(g + 1) * 4, tt * 128:(tt + 1) * 128])
        xT_done_ev = P.c("ev")

        P.wl("vector", "qtg")
        wq_ld = []
        P.do("sync", lambda h, s: h.dma_start(out=wraw, in_=Wq[0:128, :]), "wq", 16)
        wT3 = None
        last_qspill_gq = 0
        last_norm_mm = 0
        last_q_square = 0
        for dqc in range(16):
            P.wl("tensor", "wq")
            strip16(lambda j: wraw[:, j * 128:(j + 1) * 128],
                    lambda g: wTb[:, g * 512:(g + 1) * 512])
            wqT_done = P.c("ev")
            P.w("sync", "mm", P.c("mm"))
            if dqc + 1 < 16:
                P.do("sync", lambda h, s, dqc=dqc: h.dma_start(out=wraw, in_=Wq[(dqc + 1) * 128:(dqc + 2) * 128, :]), "wq", 16)
            for ch in range(2):
                b = ch
                rb = 2 + ch
                P.w("tensor", "ev", wqT_done)
                P.bank_w(b)
                for hc in range(HC):
                    P.do("tensor", lambda h, s, hc=hc, b=b, ch=ch:
                         h.matmul(pb(b), wTb[:, hc * 128:(hc + 1) * 128],
                                  AR1[:, hc * 1024 + ch * 512:hc * 1024 + ch * 512 + 512],
                                  start=(hc == 0), stop=(hc == HC - 1)), "mm")
                P.bank_wrote(b)
                P.bank_r("vector", b)
                P.w("vector", "mm", P.c("mm"))
                P.do("vector", lambda h, s, dqc=dqc, b=b:
                     h.tensor_scalar(qsS, pb(b), bqS()[:, dqc:dqc + 1], kwcolS()[:, dqc:dqc + 1], ALU.add, ALU.mult), "ev")
                P.bank_read("vector", b)
                ev_qs = P.c("ev")
                P.w("tensor", "ev", ev_qs)
                P.bank_w(rb)
                P.do("tensor", lambda h, s, rb=rb: h.matmul(pb(rb), PiS(), qsS, start=True, stop=True), "mm")
                P.bank_wrote(rb)
                mm_rot = P.c("mm")
                P.do("vector", lambda h, s, ch=ch: h.tensor_mul(m1S, qsS, qcosS(ch)), "ev")
                P.bank_r("vector", rb)
                P.do("vector", lambda h, s, rb=rb, ch=ch: h.tensor_mul(m2S, pb(rb), qsinS(ch)), "ev")
                P.bank_read("vector", rb)
                P.wl("vector", "qsp")
                P.w("vector", "sc", last_q_square)
                P.do("vector", lambda h, s: h.tensor_add(qtS, m1S, m2S), "ev")
                ev_qt = P.c("ev")
                P.w("gpsimd", "ev", ev_qt)
                for hh in range(2):
                    hq = 2 * dqc + hh
                    kvh = hq // 4
                    qh = hq % 4
                    par = kvh % 2
                    r0 = ch * 1024 + kvh * 128 + par * 64
                    P.do("gpsimd", lambda h, s, r0=r0, qh=qh, hh=hh:
                         h.dma_start(out=qpad[r0:r0 + 64, qh * 512:(qh + 1) * 512],
                                     in_=qtS[hh * 64:(hh + 1) * 64, :]), "qsp", 16)

                P.w("scalar", "ev", ev_qt)
                P.w("scalar", "mm", last_norm_mm)
                P.do("scalar", lambda h, s: h.activation(sqS, qtS, AF.Square), "sc")
                sc_sq = P.c("sc")
                last_q_square = sc_sq

                P.w("tensor", "sc", sc_sq)
                P.bank_w(6)
                P.do("tensor", lambda h, s: h.matmul(pb(6, 0, 512, 0, 2), ones2S(), sqS, start=True, stop=True), "mm")
                P.bank_wrote(6)
                last_norm_mm = P.c("mm")
                P.bank_r("vector", 6)
                P.wl("vector", "q2s")
                P.do("vector", lambda h, s: h.tensor_copy(q2stgS(), pb(6, 0, 512, 0, 2)), "ev")
                P.bank_read("vector", 6)
                ev_q2 = P.c("ev")
                P.w("sync", "ev", ev_q2)
                P.do("sync", lambda h, s, ch=ch, dqc=dqc: h.dma_start(out=q2d[ch, dqc], in_=q2stgS()), "q2s", 16)



        P.w("sync", "ev", xT_done_ev)
        P.do("sync", lambda h, s: h.dma_start(out=xTsp[:], in_=AR1[:]), "xsp", 16)

        P.barrier()
        WkT3 = AR2[:].rearrange("p (hc q) -> p hc q", q=512)
        for i in range(4):
            P.w("sync", "mm", P.c("mm")) if i > 0 else None
            P.do("sync", lambda h, s, i=i: h.dma_start(out=AR1[:, 0:2048], in_=Wk[i * 128:(i + 1) * 128, :]), "wq", 16)
            P.wl("tensor", "wq")
            strip16(lambda j: AR1[:, j * 128:(j + 1) * 128],
                    lambda g, i=i: WkT3[:, g * 4:(g + 1) * 4, i * 128:(i + 1) * 128])
        for i in range(4):
            P.w("sync", "mm", P.c("mm"))
            P.do("sync", lambda h, s, i=i: h.dma_start(out=AR1[:, 0:2048], in_=Wv[i * 128:(i + 1) * 128, :]), "wq", 16)
            P.wl("tensor", "wq")
            strip16(lambda j: AR1[:, j * 128:(j + 1) * 128],
                    lambda g, i=i: WkT3[:, 16 + g * 4:16 + (g + 1) * 4, i * 128:(i + 1) * 128])
        wkvT_done = P.c("ev")
        WkTs = lambda hc, dkc: AR2[:, hc * 512 + dkc * 128:hc * 512 + (dkc + 1) * 128]
        WvTs = lambda hc: AR2[:, 8192 + hc * 512:8192 + (hc + 1) * 512]

        kscr = AR3[:, 0:512]
        km1 = AR3[:, 512:1024]
        km2 = AR3[:, 1024:1536]
        ktil = AR3[:, 1536:2048]
        ksq = AR3[:, 2048:2560]
        vscr = lambda kt: AR3[:, 2560 + kt * 520:2560 + (kt + 1) * 520]
        vscr3 = lambda kt: AR3[:, 2560 + kt * 520:2560 + (kt + 1) * 520].rearrange("p (kv e) -> p kv e", e=65)
        P.wl("vector", "xa", "xb", "wq")
        for kt in range(4):
            P.do("vector", lambda h, s, kt=kt: h.memset(vscr3(kt)[:, :, 64:65], 1.0), "ev")

        padb = AR3[:, 7200:7712]
        pones = AR3[0:1, 7712:8224]
        P.do("vector", lambda h, s: h.memset(padb, 0.0), "ev")
        P.do("vector", lambda h, s: h.memset(pones, 1.0), "ev")
        ev_pad = P.c("ev")
        P.w("gpsimd", "ev", ev_pad)
        for u_ in range(NU):
            for kv_ in range(KV):
                if kv_ % 2 == 0:
                    P.do("gpsimd", lambda h, s, u_=u_, kv_=kv_: h.dma_start(
                        out=ksp[u_, kv_ * 128 + 64:kv_ * 128 + 65, :], in_=pones), "pi", 16)
                    P.do("gpsimd", lambda h, s, u_=u_, kv_=kv_: h.dma_start(
                        out=ksp[u_, kv_ * 128 + 65:kv_ * 128 + 128, :], in_=padb[0:63, :]), "pi", 16)
                else:
                    P.do("gpsimd", lambda h, s, u_=u_, kv_=kv_: h.dma_start(
                        out=ksp[u_, kv_ * 128:kv_ * 128 + 63, :], in_=padb[0:63, :]), "pi", 16)
                    P.do("gpsimd", lambda h, s, u_=u_, kv_=kv_: h.dma_start(
                        out=ksp[u_, kv_ * 128 + 63:kv_ * 128 + 64, :], in_=pones), "pi", 16)
        xkT = lambda hc: AR1[:, 4096 + hc * 512:4096 + (hc + 1) * 512]
        xkT3 = AR1[:, 4096:12288].rearrange("p (hc t) -> p hc t", t=512)
        last_ksp_gq = 0
        last_vsp_gq = 0
        last_knorm_mm = 0
        last_k_square = 0
        for u in range(NU):
            P.wl("gpsimd", "tsp")
            P.w("gpsimd", "ev", P.c("ev")) if u > 0 else None
            P.do("gpsimd", lambda h, s, u=u: h.indirect_dma_start(
                ucosS(), None, cosd_v, IndirectOffsetOnAxis(ap=IDX[:, 72 + u:73 + u], axis=0)), "utg", 16)
            P.do("gpsimd", lambda h, s, u=u: h.indirect_dma_start(
                usinS(), None, sind_v, IndirectOffsetOnAxis(ap=IDX[:, 72 + u:73 + u], axis=0)), "utg", 16)
            utab = None

            for tt in range(4):
                P.w("sync", "mm", P.c("mm"))
                P.do("sync", lambda h, s, u=u, tt=tt: h.dma_start(
                    out=AR1[:, (tt % 2) * 2048:(tt % 2) * 2048 + 2048],
                    in_=xkeys[u * 512 + tt * 128:u * 512 + (tt + 1) * 128, :]), "xa" if tt % 2 == 0 else "xb", 16)
                P.wl("tensor", "xa" if tt % 2 == 0 else "xb")
                strip16(lambda j, tt=tt: AR1[:, (tt % 2) * 2048 + j * 128:(tt % 2) * 2048 + (j + 1) * 128],
                        lambda g, tt=tt: xkT3[:, g * 4:(g + 1) * 4, tt * 128:(tt + 1) * 128])
            xkT_ev = P.c("ev")

            P.w("tensor", "ev", xkT_ev)
            P.wl("vector", "utg")
            for dkc in range(4):
                b = dkc % 2
                P.bank_w(b)
                for hc in range(HC):
                    P.do("tensor", lambda h, s, hc=hc, dkc=dkc, b=b:
                         h.matmul(pb(b), WkTs(hc, dkc), xkT(hc), start=(hc == 0), stop=(hc == HC - 1)), "mm")
                P.bank_wrote(b)
                P.bank_r("vector", b)
                P.w("vector", "mm", P.c("mm"))
                P.do("vector", lambda h, s, dkc=dkc, b=b:
                     h.tensor_scalar_add(kscr, pb(b), bkS()[:, dkc:dkc + 1]), "ev")
                P.bank_read("vector", b)
                ev_k = P.c("ev")
                P.w("tensor", "ev", ev_k)
                P.bank_w(2)
                P.do("tensor", lambda h, s: h.matmul(pb(2), PiS(), kscr, start=True, stop=True), "mm")
                P.bank_wrote(2)
                P.do("vector", lambda h, s: h.tensor_mul(km1, kscr, ucosS()), "ev")
                P.bank_r("vector", 2)
                P.do("vector", lambda h, s: h.tensor_mul(km2, pb(2), usinS()), "ev")
                P.bank_read("vector", 2)
                P.wl("vector", "sp")
                P.w("vector", "sc", last_k_square)
                P.do("vector", lambda h, s: h.tensor_add(ktil, km1, km2), "ev")
                ev_kt = P.c("ev")
                P.w("gpsimd", "ev", ev_kt)
                P.do("gpsimd", lambda h, s, u=u, dkc=dkc: h.dma_start(
                    out=ksp[u, (2 * dkc) * 128:(2 * dkc) * 128 + 64, :], in_=ktil[0:64, :]), "sp", 16)
                P.do("gpsimd", lambda h, s, u=u, dkc=dkc: h.dma_start(
                    out=ksp[u, (2 * dkc + 1) * 128 + 64:(2 * dkc + 1) * 128 + 128, :], in_=ktil[64:128, :]), "sp", 16)

                P.w("scalar", "ev", ev_kt)
                P.w("scalar", "mm", last_knorm_mm)
                P.do("scalar", lambda h, s: h.activation(ksq, ktil, AF.Square), "sc")
                sc_k = P.c("sc")
                last_k_square = sc_k
                P.w("tensor", "sc", sc_k)
                P.bank_w(6)
                P.do("tensor", lambda h, s: h.matmul(pb(6, 0, 512, 0, 2), ones2S(), ksq, start=True, stop=True), "mm")
                P.bank_wrote(6)
                P.bank_r("vector", 6)
                P.do("vector", lambda h, s: h.reduce_max(tmp2S()[0:2, 1:2], pb(6, 0, 512, 0, 2), axis=AX.X), "ev")
                P.bank_read("vector", 6)
                P.w("tensor", "ev", P.c("ev"))
                P.bank_w(7)
                P.do("tensor", lambda h, s: h.transpose(pb(7, 0, 2, 0, 1), tmp2S()[0:2, 1:2], identS()[0:2, 0:2]), "mm")
                P.bank_wrote(7)
                last_knorm_mm = P.c("mm")
                P.bank_r("vector", 7)
                P.do("vector", lambda h, s, dkc=dkc: h.tensor_max(Mk2S()[0:1, 2 * dkc:2 * dkc + 2],
                                                                  Mk2S()[0:1, 2 * dkc:2 * dkc + 2],
                                                                  pb(7, 0, 2, 0, 1)), "ev")
                P.drain("vector")
                P.bank_read("vector", 7)
            for kt in range(4):
                b = kt % 2
                P.bank_w(b)
                for hc in range(HC):
                    P.do("tensor", lambda h, s, hc=hc, kt=kt, b=b:
                         h.matmul(pb(b), xkT(hc)[:, kt * 128:(kt + 1) * 128], WvTs(hc),
                                  start=(hc == 0), stop=(hc == HC - 1)), "mm")
                P.bank_wrote(b)
                P.bank_r("vector", b)
                P.wl("vector", "sp")
                P.do("vector", lambda h, s, kt=kt, b=b:
                     h.tensor_copy(vscr3(kt)[:, :, 0:64], pb(b).rearrange("p (kv dv) -> p kv dv", dv=64)), "ev")
                P.bank_read("vector", b)
                ev_v = P.c("ev")
                P.w("gpsimd", "ev", ev_v)
                P.do("gpsimd", lambda h, s, u=u, kt=kt: h.dma_start(out=vsp[u, kt * 128:(kt + 1) * 128, :], in_=vscr(kt)), "sp", 16)



        P.drain("vector")
        crow = lambda: A4[0:1, 1024:3072]
        P.wl("sync", "q2s", "qsp", "utg", "qtg")
        for ch in range(2):
            for kv in range(KV):
                P.w("sync", "ev", P.c("ev"))
                P.w("sync", "sc", P.c("sc"))
                P.wl("sync", "cw")
                P.do("sync", lambda h, s, ch=ch, kv=kv: h.dma_start(
                    out=crow(), in_=q2d[ch, 2 * kv:2 * kv + 2]), "crl", 16)
                P.wl("vector", "crl")
                P.do("vector", lambda h, s, kv=kv: h.tensor_single_scalar(
                    crow(), crow(), Mk2S()[0:1, kv:kv + 1], ALU.mult), "ev")
                P.drain("vector")
                ev_c1 = P.c("ev")
                P.w("scalar", "ev", ev_c1)
                P.do("scalar", lambda h, s: h.activation(crow(), crow(), AF.Sqrt), "sc")
                sc_c = P.c("sc")
                P.w("vector", "sc", sc_c)
                P.do("vector", lambda h, s: h.tensor_scalar_add(crow(), crow(), -70.0), "ev")
                P.drain("vector")
                P.do("vector", lambda h, s: h.tensor_scalar_max(crow(), crow(), 0.0), "ev")
                P.drain("vector")
                P.do("vector", lambda h, s: h.tensor_scalar_mul(crow(), crow(), -1.0), "ev")
                P.drain("vector")
                ev_c2 = P.c("ev")
                P.w("gpsimd", "ev", ev_c2)
                r_c = 64 if kv % 2 == 0 else 63
                P.do("gpsimd", lambda h, s, ch=ch, kv=kv, r_c=r_c: h.dma_start(
                    out=qpad[ch * 1024 + kv * 128 + r_c:ch * 1024 + kv * 128 + r_c + 1, :],
                    in_=crow()), "cw", 16)

        P.barrier()
        Kslot = AR1[:, 0:4096]
        Kc = lambda kv: Kslot[:, kv * 512:(kv + 1) * 512]
        Vslot = lambda i: AR1[:, 4096 + (i % 2) * 2080:4096 + (i % 2) * 2080 + 2080]
        Vc = lambda i, kt: Vslot(i)[:, kt * 520:(kt + 1) * 520]
        qslot = lambda i: AR1[:, 8256 + (i % 2) * 2048:8256 + (i % 2) * 2048 + 2048]
        Pbuf = lambda kt: AR1[:, 12352 + kt * 512:12352 + (kt + 1) * 512]
        oaccA = AR2[0:65, :]
        oaccB = AR3[0:65, :]

        P.do("vector", lambda h, s: h.memset(oaccA, 0.0), "ev")
        P.do("vector", lambda h, s: h.memset(oaccB, 0.0), "ev")
        oacc_init = P.c("ev")

        kspv = lambda u: ksp[u].rearrange("(kv p) k -> p kv k", p=128)
        vspv = lambda u: vsp[u].rearrange("(kt p) e -> p kt e", p=128)
        P.wl("sync", "sp", "pi", "cw")
        P.do("sync", lambda h, s: h.dma_start(out=Kslot, in_=kspv(0)), "ks", 16)
        P.do("sync", lambda h, s: h.dma_start(out=Vslot(0), in_=vspv(0)), "va", 16)
        qpad_v = qpad[:]
        P.wl("gpsimd", "cw", "qz", "qsp")
        P.do("gpsimd", lambda h, s: h.indirect_dma_start(
            qslot(0), None, qpad_v, IndirectOffsetOnAxis(ap=IDX[:, 0:1], axis=0)), "qa", 16)

        exp_bank_sc = {4: 0, 5: 0, 6: 0, 7: 0}
        omm_of_p = {}
        stt_hist = []
        P.w("tensor", "ev", oacc_init)
        step = 0
        for u in range(NU):
            diag = (u >= 7)
            if u == 7:
                P.w("vector", "mm", P.c("mm"))
                for kt in range(1, 4):
                    P.do("vector", lambda h, s, kt=kt: h.memset(Pbuf(kt)[:, 0:kt * 128], 0.0), "ev")
                pz = P.c("ev")
                P.w("scalar", "ev", pz)
                P.w("tensor", "ev", pz)
            if u > 0:
                P.w("sync", "mm", P.c("mm"))
                P.do("sync", lambda h, s, u=u: h.dma_start(out=Kslot, in_=kspv(u)), "ks", 16)
            if u + 1 < NU:
                P.w("sync", "mm", P.c("mm")) if u > 0 else None
                P.do("sync", lambda h, s, u=u: h.dma_start(out=Vslot(u + 1), in_=vspv(u + 1)),
                     "va" if (u + 1) % 2 == 0 else "vb", 16)
            P.wl("tensor", "va" if u % 2 == 0 else "vb")
            P.wl("tensor", "ks")
            for kv in range(KV):
                qi = u * KV + kv
                if qi + 1 < NU * KV:
                    P.w("gpsimd", "mm", P.c("mm")) if qi >= 1 else None
                    P.do("gpsimd", lambda h, s, qi=qi: h.indirect_dma_start(
                        qslot(qi + 1), None, qpad_v, IndirectOffsetOnAxis(ap=IDX[:, qi + 1:qi + 2], axis=0)),
                        "qa" if (qi + 1) % 2 == 0 else "qb", 16)
                P.wl("tensor", "qa" if qi % 2 == 0 else "qb")
                for qh in range(4):
                    hq = kv * 4 + qh
                    ob = 2 + (step % 2)
                    mm_s_l = []
                    for kt in range(4):
                        sb = 4 + kt
                        P.w("tensor", "sc", exp_bank_sc[sb])
                        P.do("tensor", lambda h, s, kv=kv, qh=qh, kt=kt, sb=sb, qi=qi:
                             h.matmul(pb(sb), Kc(kv)[:, kt * 128:(kt + 1) * 128],
                                      qslot(qi)[:, qh * 512:(qh + 1) * 512],
                                      start=True, stop=True), "mm")
                        mm_s_l.append(P.c("mm"))
                    for kt in range(4):
                        sb = 4 + kt
                        lo = kt * 128 if diag else 0
                        P.w("scalar", "mm", max(mm_s_l[kt], omm_of_p.get(kt, 0)))
                        P.do("scalar", lambda h, s, kt=kt, sb=sb, lo=lo:
                             h.activation(Pbuf(kt)[:, lo:512], pb(sb, lo, 512), AF.Exp), "sc")
                        exp_bank_sc[sb] = P.c("sc")
                        sc_exp = P.c("sc")
                        if diag:
                            P.w("vector", "sc", sc_exp)
                            P.do("vector", lambda h, s, kt=kt, lo=lo:
                                 h.tensor_mul(Pbuf(kt)[:, lo:lo + 128], Pbuf(kt)[:, lo:lo + 128], triS()), "ev")
                            P.w("tensor", "ev", P.c("ev"))
                        else:
                            P.w("tensor", "sc", sc_exp)
                        if kt == 0:
                            if len(stt_hist) >= 2:
                                P.w("tensor", "ev", stt_hist[-2])
                        P.do("tensor", lambda h, s, u=u, kv=kv, kt=kt, ob=ob:
                             h.matmul(pb(ob, 0, 512, 0, 65), Vc(u, kt)[:, kv * 65:(kv + 1) * 65],
                                      Pbuf(kt), start=(kt == 0), stop=(kt == 3)), "mm")
                        omm_of_p[kt] = P.c("mm")
                    mm_o = P.c("mm")
                    P.w("vector", "mm", mm_o)
                    P.do("vector", lambda h, s, u=u, hq=hq, ob=ob:
                         h.scalar_tensor_tensor(oaccA[:, hq * 512:(hq + 1) * 512], pb(ob, 0, 512, 0, 65),
                                                wcolS()[0:65, u:u + 1], oaccA[:, hq * 512:(hq + 1) * 512],
                                                ALU.mult, ALU.add), "ev")
                    P.do("vector", lambda h, s, u=u, hq=hq, ob=ob:
                         h.scalar_tensor_tensor(oaccB[:, hq * 512:(hq + 1) * 512], pb(ob, 0, 512, 0, 65),
                                                wcolS()[0:65, NU + u:NU + u + 1], oaccB[:, hq * 512:(hq + 1) * 512],
                                                ALU.mult, ALU.add), "ev")
                    stt_hist.append(P.c("ev"))
                    step += 1

        P.barrier()
        if debug:
            P.do("sync", lambda h, s: h.dma_start(out=d_oA, in_=oaccA), "dbg", 16)
            P.do("sync", lambda h, s: h.dma_start(out=d_oB, in_=oaccB), "dbg", 16)
            P.do("sync", lambda h, s: h.dma_start(out=d_ncr, in_=crS()), "dbg", 16)
        onT = AR1
        P.do("vector", lambda h, s: h.tensor_scalar_max(oaccA[64:65, :], oaccA[64:65, :], 1e-37), "ev")
        P.do("vector", lambda h, s: h.tensor_scalar_max(oaccB[64:65, :], oaccB[64:65, :], 1e-37), "ev")
        P.drain("vector")
        P.w("tensor", "ev", P.c("ev"))
        bc65S = lambda: A4[0:65, 902:966]
        lbr = lambda: A4[0:64, 966:1478]
        onscr = lambda: A4[0:64, 1478:1990]
        P.do("sync", lambda h, s: h.dma_start(out=bc65S(), in_=bc65), "boot2", 16)
        P.wl("tensor", "boot2")
        last_on_gq = 0
        for ch in range(2):
            oacc = oaccA if ch == 0 else oaccB
            for hq in range(NH):
                kvh = hq // 4
                fc = hq // 2
                hh = hq % 2
                b = hq % 2
                P.bank_w(b)
                P.do("tensor", lambda h, s, oacc=oacc, hq=hq, b=b:
                     h.matmul(pb(b, 0, 512, 0, 64), bc65S(), oacc[:, hq * 512:(hq + 1) * 512], start=True, stop=True), "mm")
                P.bank_wrote(b)
                P.bank_r("vector", b)
                P.w("vector", "mm", P.c("mm"))
                P.do("vector", lambda h, s, b=b: h.reciprocal(lbr(), pb(b, 0, 512, 0, 64)), "ev")
                P.bank_read("vector", b)
                P.wl("vector", "os")
                P.do("vector", lambda h, s, oacc=oacc, hq=hq:
                     h.tensor_mul(onscr(), oacc[0:64, hq * 512:(hq + 1) * 512], lbr()), "ev")
                P.do("vector", lambda h, s, kvh=kvh: h.tensor_scalar_add(onscr(), onscr(), bvS()[:, kvh:kvh + 1]), "ev")
                ev_on = P.c("ev")
                if hh == 0:
                    P.do("vector", lambda h, s, fc=fc, ch=ch: h.tensor_copy(
                        onT[0:64, fc * 1024 + ch * 512:fc * 1024 + ch * 512 + 512], onscr()), "ev")
                    last_on_ev = P.c("ev")
                else:
                    P.w("gpsimd", "ev", ev_on)
                    P.do("gpsimd", lambda h, s, fc=fc, ch=ch: h.dma_start(
                        out=onT[64:128, fc * 1024 + ch * 512:fc * 1024 + ch * 512 + 512], in_=onscr()), "os", 16)
        onT_ev = P.c("ev")

        P.barrier()
        wostg = AR3[:, 0:2048]
        woT = AR3[:, 4096:6144]
        xchunk = lambda i: AR3[:, 8192 + (i % 2) * 512:8192 + (i % 2) * 512 + 512]
        sqs2 = AR3[:, 9216:9728]
        sqs3 = AR3[:, 9728:10240]
        wo_ld = []
        P.wl("sync", "xsp")
        P.do("sync", lambda h, s: h.dma_start(out=wostg, in_=Wo[0:128, :]), "wq", 16)
        y1_hist = []
        for fo in range(16):
            P.wl("tensor", "wq")
            strip16(lambda j: wostg[:, j * 128:(j + 1) * 128],
                    lambda g: woT[:, g * 512:(g + 1) * 512])
            woT_done = P.c("ev")
            P.w("sync", "mm", P.c("mm"))
            if fo + 1 < 16:
                P.do("sync", lambda h, s, fo=fo: h.dma_start(out=wostg, in_=Wo[(fo + 1) * 128:(fo + 2) * 128, :]), "wq", 16)
            for ch in range(2):
                if len(y1_hist) >= 2:
                    P.w("sync", "ev", y1_hist[-2])
                P.do("sync", lambda h, s, fo=fo, ch=ch: h.dma_start(
                    out=xchunk(ch), in_=xTsp[:, fo * 1024 + ch * 512:fo * 1024 + ch * 512 + 512]), "xc" if ch == 0 else "xd", 16)
                b = ch
                P.w("tensor", "ev", woT_done)
                P.bank_w(b)
                for fi in range(16):
                    P.do("tensor", lambda h, s, fo=fo, fi=fi, ch=ch, b=b:
                         h.matmul(pb(b), woT[:, fi * 128:(fi + 1) * 128],
                                  onT[:, fi * 1024 + ch * 512:fi * 1024 + ch * 512 + 512],
                                  start=(fi == 0), stop=(fi == 15)), "mm")
                P.bank_wrote(b)
                P.bank_r("vector", b)
                P.wl("vector", "xc" if ch == 0 else "xd")
                P.do("vector", lambda h, s, fo=fo, ch=ch, b=b:
                     h.scalar_tensor_tensor(AR2[:, fo * 1024 + ch * 512:fo * 1024 + ch * 512 + 512],
                                            pb(b), boS()[:, fo:fo + 1], xchunk(ch),
                                            ALU.add, ALU.add), "ev")
                P.bank_read("vector", b)
                y1_hist.append(P.c("ev"))
        y1_done = P.c("ev")
        if debug:
            P.w("sync", "ev", y1_done)
            P.do("sync", lambda h, s: h.dma_start(out=d_onT, in_=onT[:]), "dbg", 16)
            P.do("sync", lambda h, s: h.dma_start(out=d_y1, in_=AR2[:]), "dbg", 16)
            P.wl("tensor", "dbg")
            P.wl("vector", "dbg")
            P.wl("scalar", "dbg")

        murow = lambda: A4[0:1, 1024:1536]
        isrow = lambda: A4[0:1, 1536:2048]
        r1row = lambda: A4[0:1, 2048:2560]
        P.w("tensor", "ev", y1_done)
        for ch in range(2):
            P.bank_w(2)
            for hc in range(HC):
                P.do("tensor", lambda h, s, hc=hc, ch=ch: h.matmul(pb(2, 0, 512, 0, 1), onescS(),
                                                                   AR2[:, hc * 1024 + ch * 512:hc * 1024 + ch * 512 + 512],
                                                                   start=(hc == 0), stop=(hc == HC - 1)), "mm")
            P.bank_wrote(2)
            last_sq_mm = 0
            P.bank_w(3)
            ev_now = P.c("ev")
            for hc in range(HC):
                P.w("scalar", "ev", ev_now)
                P.w("scalar", "mm", last_sq_mm)
                P.do("scalar", lambda h, s, hc=hc, ch=ch: h.activation(sqs2, AR2[:, hc * 1024 + ch * 512:hc * 1024 + ch * 512 + 512], AF.Square), "sc")
                sc_s = P.c("sc")
                P.w("tensor", "sc", sc_s)
                P.do("tensor", lambda h, s, hc=hc: h.matmul(pb(3, 0, 512, 0, 1), onescS(), sqs2,
                                                            start=(hc == 0), stop=(hc == HC - 1)), "mm")
                last_sq_mm = P.c("mm")
            P.bank_wrote(3)
            P.bank_r("vector", 2)
            P.bank_r("vector", 3)
            P.do("vector", lambda h, s: h.tensor_scalar_mul(murow(), pb(2, 0, 512, 0, 1), 1.0 / H), "ev")
            P.do("vector", lambda h, s: h.tensor_mul(r1row(), murow(), murow()), "ev")
            P.do("vector", lambda h, s: h.scalar_tensor_tensor(isrow(), pb(3, 0, 512, 0, 1), 1.0 / H, r1row(), ALU.mult, ALU.subtract), "ev")
            P.bank_read("vector", 2)
            P.bank_read("vector", 3)
            P.do("vector", lambda h, s: h.tensor_scalar_add(isrow(), isrow(), 1e-5), "ev")
            P.w("scalar", "ev", P.c("ev"))
            P.do("scalar", lambda h, s: h.activation(isrow(), isrow(), AF.Sqrt), "sc")
            P.w("vector", "sc", P.c("sc"))
            P.do("vector", lambda h, s: h.reciprocal(isrow(), isrow()), "ev")
            P.w("tensor", "ev", P.c("ev"))
            P.bank_w(2); P.bank_w(3)
            P.do("tensor", lambda h, s: h.matmul(pb(2), onesrS(), murow(), start=True, stop=True), "mm")
            P.do("tensor", lambda h, s: h.matmul(pb(3), onesrS(), isrow(), start=True, stop=True), "mm")
            P.bank_wrote(2); P.bank_wrote(3)
            P.bank_r("vector", 2); P.bank_r("vector", 3)
            for hc in range(HC):
                P.do("vector", lambda h, s, hc=hc, ch=ch: h.tensor_sub(sqs3, AR2[:, hc * 1024 + ch * 512:hc * 1024 + ch * 512 + 512], pb(2)), "ev")
                P.do("vector", lambda h, s, hc=hc, ch=ch: h.tensor_mul(AR2[:, hc * 1024 + ch * 512:hc * 1024 + ch * 512 + 512], sqs3, pb(3)), "ev")
            P.bank_read("vector", 2); P.bank_read("vector", 3)
        x1_done = P.c("ev")
        if debug:
            P.w("sync", "ev", x1_done)
            P.do("sync", lambda h, s: h.dma_start(out=d_x1, in_=AR2[:]), "dbg", 16)

        P.barrier()
        gsS = lambda: A4[:, 1024:1536]
        hbS = lambda: A4[:, 1536:2048]
        AmO = [128, 256, 384, 516]
        AmF = lambda i: A4[:, AmO[i]:AmO[i] + 128]
        geS = lambda i: A4[:, 2560 + i * 128:2560 + (i + 1) * 128]
        cs0 = lambda: A4[:, 3328:3456]
        stat = lambda i: A4[:, 3456 + i:3457 + i]
        hsT = lambda: A4[:, 2048:2560]
        out2 = AR1[:, 8192:16384]
        Wdb = AR1[:, 0:8192]
        Wgb = AR3[:, 0:8192]
        Wub = AR3[:, 8192:16384]

        for half in range(2):
            ch = half
            P.w("vector", "mm", P.c("mm"))
            P.wl("vector", "os", "st")
            P.do("vector", lambda h, s: h.memset(out2, 0.0), "ev")
            out2_init = P.c("ev")
            if half == 0:
                ld_gu = {}
                ld_d = {}
            def load_gu(ib):
                P.w("sync", "mm", P.c("mm"))
                for hc2 in range(HC):
                    P.do("sync", lambda h, s, ib=ib, hc2=hc2: h.dma_start(
                        out=Wgb[:, hc2 * 512:(hc2 + 1) * 512],
                        in_=Wg[hc2 * 128:(hc2 + 1) * 128, ib * 512:(ib + 1) * 512]), "fw", 16)
                    P.do("sync", lambda h, s, ib=ib, hc2=hc2: h.dma_start(
                        out=Wub[:, hc2 * 512:(hc2 + 1) * 512],
                        in_=Wu[hc2 * 128:(hc2 + 1) * 128, ib * 512:(ib + 1) * 512]), "fw", 16)
                ld_gu[ib] = P.c("fw")
            def load_d(ib):
                P.w("sync", "mm", P.c("mm"))
                for ic in range(4):
                    P.do("sync", lambda h, s, ib=ib, ic=ic: h.dma_start(
                        out=Wdb[:, ic * 2048:(ic + 1) * 2048],
                        in_=Wd[ib * 512 + ic * 128:ib * 512 + (ic + 1) * 128, :]), "fd", 16)
                ld_d[ib] = P.c("fd")
            if half == 0:
                load_gu(0)
                load_d(0)
            pend = None
            tr_mark = 0
            def tail_head(pd):
                nonlocal tr_mark
                P.w("tensor", "ev", pd[2])
                P.bank_w(2)
                for ic in range(4):
                    P.do("tensor", lambda h, s, ic=ic: h.transpose(pb(2, ic * 128, (ic + 1) * 128), hbS()[:, ic * 128:(ic + 1) * 128], identS()), "mm")
                P.bank_wrote(2)
                tr_mark = P.c("mm")
                P.bank_r("vector", 2)
                P.do("vector", lambda h, s: h.tensor_copy(hsT(), pb(2)), "ev")
                P.bank_read("vector", 2)
                return P.c("ev")
            def tail_mms(pd, ev_hsT):
                P.w("tensor", "fd", ld_d[pd[0]])
                P.w("tensor", "ev", ev_hsT)
                for fb in range(4):
                    b = [3, 4, 6, 7][fb]
                    P.bank_w(b)
                    for ic in range(4):
                        P.do("tensor", lambda h, s, ic=ic, fb=fb, b=b:
                             h.matmul(pb(b), hsT()[:, ic * 128:(ic + 1) * 128],
                                      Wdb[:, ic * 2048 + fb * 512:ic * 2048 + (fb + 1) * 512],
                                      start=(ic == 0), stop=(ic == 3)), "mm")
                    P.bank_wrote(b)
                if pd[1] == 3 and pd[0] + 1 < 16:
                    load_d(pd[0] + 1)
            def tail_evacs(pd):
                for fb in range(4):
                    b = [3, 4, 6, 7][fb]
                    P.bank_r("vector", b)
                    P.do("vector", lambda h, s, ttp=pd[1], fb=fb, b=b:
                         h.tensor_add(out2[:, ttp * 2048 + fb * 512:ttp * 2048 + (fb + 1) * 512],
                                      out2[:, ttp * 2048 + fb * 512:ttp * 2048 + (fb + 1) * 512],
                                      pb(b)), "ev")
                    P.bank_read("vector", b)
            ev_mult_prev = 0
            for ib in range(16):
                for tt in range(4):
                    x1s = lambda hc2, tt=tt, ch=ch: AR2[:, hc2 * 1024 + ch * 512 + tt * 128:hc2 * 1024 + ch * 512 + tt * 128 + 128]
                    hs_ev = None
                    if pend is not None:
                        hs_ev = tail_head(pend)
                    if tt == 0:
                        P.w("tensor", "fw", ld_gu[ib])
                    P.bank_w(0)
                    for hc2 in range(HC):
                        P.do("tensor", lambda h, s, hc2=hc2, x1s=x1s:
                             h.matmul(pb(0), x1s(hc2), Wgb[:, hc2 * 512:(hc2 + 1) * 512],
                                      start=(hc2 == 0), stop=(hc2 == HC - 1)), "mm")
                    P.bank_wrote(0)
                    mm_g = P.c("mm")
                    P.bank_w(1)
                    for hc2 in range(HC):
                        P.do("tensor", lambda h, s, hc2=hc2, x1s=x1s:
                             h.matmul(pb(1), x1s(hc2), Wub[:, hc2 * 512:(hc2 + 1) * 512],
                                      start=(hc2 == 0), stop=(hc2 == HC - 1)), "mm")
                    P.bank_wrote(1)
                    mm_u = P.c("mm")
                    if tt == 3 and ib + 1 < 16:
                        load_gu(ib + 1)
                    if pend is not None:
                        tail_mms(pend, hs_ev)
                    P.bank_r("scalar", 0)
                    P.w("scalar", "ev", ev_mult_prev)
                    P.do("scalar", lambda h, s: h.activation(gsS(), pb(0), AF.Silu), "sc")
                    P.bank_read("scalar", 0)
                    sc_g = P.c("sc")
                    P.bank_r("vector", 1)
                    P.w("vector", "sc", sc_g)
                    P.w("vector", "mm", tr_mark)
                    P.do("vector", lambda h, s: h.tensor_mul(hbS(), gsS(), pb(1)), "ev")
                    P.bank_read("vector", 1)
                    ev_h = P.c("ev")
                    ev_mult_prev = ev_h
                    P.w("scalar", "ev", ev_h)
                    hb3 = hbS().rearrange("p (g f) -> p g f", f=4)
                    for f_ in range(4):
                        P.do("scalar", lambda h, s, f_=f_: h.activation(AmF(f_), hb3[:, :, f_:f_ + 1], AF.Abs), "sc")
                    sc_a = P.c("sc")
                    P.w("vector", "sc", sc_a)
                    Am = AmF
                    hv = lambda i: hb3[:, :, i:i + 1]
                    pairs = [(0, 1), (0, 2), (0, 3), (1, 2), (1, 3), (2, 3)]
                    for pi_, (a, b_) in enumerate(pairs):
                        P.do("vector", lambda h, s, pi_=pi_, a=a, b_=b_: h.tensor_tensor(geS(pi_), Am(a), Am(b_), ALU.is_ge), "ev")
                    P.do("vector", lambda h, s: h.tensor_add(cs0(), geS(0), geS(1)), "ev")
                    P.do("vector", lambda h, s: h.tensor_add(cs0(), cs0(), geS(2)), "ev")
                    P.do("vector", lambda h, s: h.scalar_tensor_tensor(hv(0), cs0().rearrange("p (g o) -> p g o", o=1), 2.0, hv(0), ALU.is_ge, ALU.mult), "ev")
                    P.do("vector", lambda h, s: h.tensor_add(cs0(), geS(3), geS(4)), "ev")
                    P.do("vector", lambda h, s: h.tensor_sub(cs0(), cs0(), geS(0)), "ev")
                    P.do("vector", lambda h, s: h.scalar_tensor_tensor(hv(1), cs0().rearrange("p (g o) -> p g o", o=1), 1.0, hv(1), ALU.is_ge, ALU.mult), "ev")
                    P.do("vector", lambda h, s: h.tensor_sub(cs0(), geS(5), geS(1)), "ev")
                    P.do("vector", lambda h, s: h.tensor_sub(cs0(), cs0(), geS(3)), "ev")
                    P.do("vector", lambda h, s: h.scalar_tensor_tensor(hv(2), cs0().rearrange("p (g o) -> p g o", o=1), 0.0, hv(2), ALU.is_ge, ALU.mult), "ev")
                    P.do("vector", lambda h, s: h.tensor_add(cs0(), geS(2), geS(4)), "ev")
                    P.do("vector", lambda h, s: h.tensor_add(cs0(), cs0(), geS(5)), "ev")
                    P.do("vector", lambda h, s: h.scalar_tensor_tensor(hv(3), cs0().rearrange("p (g o) -> p g o", o=1), 1.0, hv(3), ALU.is_le, ALU.mult), "ev")
                    ev_mask = P.c("ev")
                    if debug and half == 0 and ib == 0 and tt == 0:
                        P.w("sync", "ev", ev_mask)
                        P.do("sync", lambda h, s: h.dma_start(out=d_h0, in_=hbS()), "dbg", 16)
                        P.wl("vector", "dbg")
                        P.wl("scalar", "dbg")
                    if pend is not None:
                        tail_evacs(pend)
                    pend = (ib, tt, ev_mask)
            hs_ev = tail_head(pend)
            tail_mms(pend, hs_ev)
            if half == 0:
                load_gu(0)
                load_d(0)
            tail_evacs(pend)
            pend = None
            for tt in range(4):
                for hc2 in range(HC):
                    b = 4 + hc2 % 2
                    P.bank_w(b)
                    P.do("tensor", lambda h, s, hc2=hc2, tt=tt, ch=ch, b=b:
                         h.transpose(pb(b, 0, 128), AR2[:, hc2 * 1024 + ch * 512 + tt * 128:hc2 * 1024 + ch * 512 + tt * 128 + 128], identS()), "mm")
                    P.bank_wrote(b)
                    P.bank_r("vector", b)
                    P.do("vector", lambda h, s, hc2=hc2, tt=tt, b=b:
                         h.tensor_add(out2[:, tt * 2048 + hc2 * 128:tt * 2048 + (hc2 + 1) * 128],
                                      out2[:, tt * 2048 + hc2 * 128:tt * 2048 + (hc2 + 1) * 128],
                                      pb(b, 0, 128)), "ev")
                    P.bank_read("vector", b)
                P.do("vector", lambda h, s, tt=tt: h.reduce_sum(stat(0), out2[:, tt * 2048:(tt + 1) * 2048], axis=AX.X), "ev")
                ev_sum = P.c("ev")
                P.w("scalar", "ev", ev_sum)
                for q4 in range(4):
                    P.do("scalar", lambda h, s, tt=tt, q4=q4: h.activation(
                        gsS(), out2[:, tt * 2048 + q4 * 512:tt * 2048 + (q4 + 1) * 512], AF.Square,
                        accum_out=stat(1 + q4)), "sc")
                sc_ss = P.c("sc")
                P.w("vector", "sc", sc_ss)
                P.do("vector", lambda h, s: h.reduce_sum(stat(5), A4[:, 3457:3461], axis=AX.X), "ev")
                P.drain("vector")
                P.do("vector", lambda h, s: h.tensor_scalar_mul(stat(6), stat(0), 1.0 / H), "ev")
                P.drain("vector")
                P.do("vector", lambda h, s: h.tensor_mul(stat(7), stat(6), stat(6)), "ev")
                P.drain("vector")
                P.do("vector", lambda h, s: h.scalar_tensor_tensor(stat(8), stat(5), 1.0 / H, stat(7), ALU.mult, ALU.subtract), "ev")
                P.drain("vector")
                P.do("vector", lambda h, s: h.tensor_scalar_add(stat(8), stat(8), 1e-5), "ev")
                P.w("scalar", "ev", P.c("ev"))
                P.do("scalar", lambda h, s: h.activation(stat(9), stat(8), AF.Sqrt), "sc")
                P.w("vector", "sc", P.c("sc"))
                P.do("vector", lambda h, s: h.reciprocal(stat(10), stat(9)), "ev")
                P.drain("vector")
                P.do("vector", lambda h, s, tt=tt: h.tensor_scalar(out2[:, tt * 2048:(tt + 1) * 2048],
                                                                   out2[:, tt * 2048:(tt + 1) * 2048],
                                                                   stat(6), stat(10),
                                                                   ALU.subtract, ALU.mult), "ev")
                ev_out = P.c("ev")
                P.w("sync", "ev", ev_out)
                P.do("sync", lambda h, s, tt=tt, half=half: h.dma_start(
                    out=y[half * 512 + tt * 128:half * 512 + (tt + 1) * 128, :],
                    in_=out2[:, tt * 2048:(tt + 1) * 2048]), "st", 16)

        for ln_ in ("st", "q2s", "qsp", "sp", "cw", "os", "xsp", "dbg"):
            P.wl("sync", ln_)

        @block.sync
        def _(h):
            for fn in P.ops["sync"]:
                fn(h, sems)

        @block.tensor
        def _(h):
            for fn in P.ops["tensor"]:
                fn(h, sems)

        @block.vector
        def _(h):
            for fn in P.ops["vector"]:
                fn(h, sems)

        @block.scalar
        def _(h):
            for fn in P.ops["scalar"]:
                fn(h, sems)

        @block.gpsimd
        def _(h):
            for fn in P.ops["gpsimd"]:
                fn(h, sems)

    return nc


def host_inputs(core, inp):
    c = core
    cA, cB = c, 7 - c
    x = np.asarray(inp["L0_attn_x"], dtype=np.float32)
    b0 = [x[0, k * 512:(k + 1) * 512] for k in range(8)]
    b1 = [x[1, k * 512:(k + 1) * 512] for k in range(8)]
    x_own = np.concatenate([b0[cA], b1[cB]], axis=0)
    pre = [b0[k] for k in range(cA)] + [b1[k] for k in range(cB)] + [b0[cA], b1[cB]]
    xkeys = np.concatenate(pre, axis=0)
    unit_pos = list(range(cA)) + list(range(cB)) + [cA, cB]
    unit_sel = [0] * cA + [1] * cB + [0, 1]

    wselr = np.zeros((1, 18), np.float32)
    for u in range(9):
        wselr[0, u] = 1.0 if unit_sel[u] == 0 else 0.0
        wselr[0, 9 + u] = 1.0 - wselr[0, u]

    p_ = np.arange(128, dtype=np.uint32)
    qgidx = np.zeros((128, 72), np.uint32)
    for u in range(9):
        for kv in range(8):
            qgidx[:, u * 8 + kv] = unit_sel[u] * 1024 + kv * 128 + p_
    tabidx = np.zeros((128, 11), np.uint32)
    for u in range(9):
        tabidx[:, u] = p_ * 8 + unit_pos[u]
    tabidx[:, 9] = p_ * 8 + cA
    tabidx[:, 10] = p_ * 8 + cB

    invf = 10000.0 ** (-(np.arange(32, dtype=np.float64) * 2) / 64.0)
    t64 = np.arange(4096, dtype=np.float64)
    ang = invf[:, None] * t64[None, :]
    nsin32 = np.round(ang / (2 * np.pi))
    ncos32 = np.round((ang + np.pi / 2) / (2 * np.pi))
    nsin = np.concatenate([nsin32, nsin32], 0).astype(np.float32)
    ncos = np.concatenate([ncos32, ncos32], 0).astype(np.float32)
    tb = np.tile(np.arange(4096, dtype=np.float32)[None, :], (64, 1))
    invf64_2 = np.concatenate([invf, invf])
    invfh = invf64_2.astype(np.float32)
    invfl = (invf64_2 - invfh.astype(np.float64)).astype(np.float32)
    invfh = invfh.reshape(64, 1); invfl = invfl.reshape(64, 1)

    ident = np.eye(128, dtype=np.float32)
    Pi = np.zeros((128, 128), np.float32)
    for i in range(128):
        g = (i // 64) * 64
        Pi[i, g + (i - g + 32) % 64] = 1.0
    E2 = np.zeros((2, 128), np.float32)
    E2[0, 0:64] = SCALE
    E2[1, 64:128] = SCALE
    onesr = np.ones((1, 128), np.float32)
    onesc = np.ones((128, 1), np.float32)
    ones2 = np.zeros((128, 2), np.float32)
    ones2[0:64, 0] = 1.0
    ones2[64:128, 1] = 1.0
    tri = (np.arange(128)[:, None] <= np.arange(128)[None, :]).astype(np.float32)
    bc65 = np.zeros((65, 64), np.float32)
    bc65[64, :] = 1.0

    return {
        "x_own": np.ascontiguousarray(x_own),
        "xkeys": np.ascontiguousarray(xkeys),
        "Wq": np.asarray(inp["L0_attn_Wq"], np.float32),
        "Wk": np.asarray(inp["L0_attn_Wk"], np.float32),
        "Wv": np.asarray(inp["L0_attn_Wv"], np.float32),
        "Wo": np.asarray(inp["L0_attn_Wo"], np.float32),
        "Wg": np.asarray(inp["L0_ffn_w_gate"], np.float32),
        "Wu": np.asarray(inp["L0_ffn_w_up"], np.float32),
        "Wd": np.asarray(inp["L0_ffn_w_down"], np.float32),
        "bqc": np.ascontiguousarray(np.asarray(inp["L0_attn_bq"], np.float32).reshape(16, 128).T),
        "bkc": np.ascontiguousarray(np.asarray(inp["L0_attn_bk"], np.float32).reshape(4, 128).T),
        "bvc": np.ascontiguousarray(np.asarray(inp["L0_attn_bv"], np.float32).reshape(8, 64).T),
        "boc": np.ascontiguousarray(np.asarray(inp["L0_attn_bo"], np.float32).reshape(16, 128).T),
        "kwpair": np.ascontiguousarray(np.asarray(inp["L0_attn_key_weights"], np.float32).reshape(16, 2).T),
        "ident": ident, "Pi": Pi, "E2": E2, "onesr": onesr, "onesc": onesc, "ones2": ones2,
        "tri": tri, "bc65": bc65, "invfh": invfh, "invfl": invfl, "tb": tb, "nsin": nsin, "ncos": ncos,
        "wselr": wselr, "qgidx": qgidx, "tabidx": tabidx,
    }


def kernel(**inp):
    nc = build_program(debug=False)
    in_maps = [host_inputs(c, inp) for c in range(8)]
    res = run_bass_kernel_spmd(nc, in_maps, core_ids=list(range(8)))
    out = np.empty((2, 4096, 2048), np.float32)
    for c in range(8):
        yv = np.asarray(res.results[c]["y"])
        out[0, c * 512:(c + 1) * 512] = yv[0:512]
        out[1, (7 - c) * 512:(8 - c) * 512] = yv[512:1024]
    return out


# version 05
# speedup vs baseline: 1.3847x; 1.0521x over previous
import numpy as np
import concourse.bass as bass
import concourse.mybir as mybir
from concourse.bass_utils import run_bass_kernel_spmd
from concourse.bass_core_rust import IndirectOffsetOnAxis

F32 = mybir.dt.float32
U32 = mybir.dt.uint32
AF = mybir.ActivationFunctionType
ALU = mybir.AluOpType
AX = mybir.AxisListType

H = 2048
HC = 16
DH = 64
NH = 32
KV = 8
BLK = 512
NU = 9
FF = 8192
SCALE = DH ** -0.5

TWO_PI = 2.0 * np.pi
C1 = float(np.float32(TWO_PI)); C2 = float(np.float32(TWO_PI - np.float64(np.float32(TWO_PI))))
LN1E4 = np.log(10000.0)
L1 = float(np.float32(LN1E4)); L2 = float(np.float32(LN1E4 - np.float64(np.float32(LN1E4))))
HP1 = float(np.float32(np.pi / 2)); HP2 = float(np.float32(np.pi / 2 - np.float64(np.float32(np.pi / 2))))

ENGINES = ("sync", "tensor", "vector", "scalar", "gpsimd")
ESEM = {"vector": "ev", "scalar": "sc", "tensor": "mm", "gpsimd": "gc", "sync": "sq"}


class Prog:
    def __init__(self):
        self.ops = {e: [] for e in ENGINES}
        self.cnt = {}
        self.bwr = {}
        self.brd = {}

    def w(self, eng, sem, val):
        if val and val > 0:
            self.ops[eng].append(lambda h, sems, s=sem, v=val: h.wait_ge(sems[s], v))

    def do(self, eng, build, sem=None, amt=1):
        if eng in ("vector", "scalar") and sem in ("ev", "sc"):
            self.ops[eng].append(lambda h, sems: h.drain())
        if sem is None:
            self.ops[eng].append(lambda h, sems, b=build: b(h, sems))
            return None
        self.cnt[sem] = self.cnt.get(sem, 0) + amt
        def fn(h, sems, b=build, s=sem, a=amt):
            b(h, sems).then_inc(sems[s], a)
        self.ops[eng].append(fn)
        return self.cnt[sem]

    def c(self, sem):
        return self.cnt.get(sem, 0)

    def wl(self, eng, *lanes):
        for ln in lanes:
            self.w(eng, ln, self.c(ln))

    def drain(self, eng):
        self.ops[eng].append(lambda h, sems: h.drain())

    def barrier(self):
        snap = dict(self.cnt)
        for e in ENGINES:
            for s, v in snap.items():
                self.w(e, s, v)
        self.bwr = {}; self.brd = {}

    def bank_w(self, b):
        g = self.brd.get(b)
        if g:
            self.w("tensor", g[0], g[1])

    def bank_wrote(self, b):
        self.bwr[b] = ("mm", self.c("mm"))

    def bank_r(self, eng, b):
        g = self.bwr.get(b)
        if g:
            self.w(eng, g[0], g[1])

    def bank_read(self, eng, b):
        self.brd[b] = (ESEM[eng], self.c(ESEM[eng]))


def build_program(debug=False):
    nc = bass.Bass("TRN2", target_bir_lowering=False)
    d = {}
    def din(name, shape, dt=F32):
        d[name] = nc.dram_tensor(name, list(shape), dt, kind="ExternalInput").ap()
        return d[name]

    x_own = din("x_own", [1024, H])
    xkeys = din("xkeys", [NU * BLK, H])
    Wq = din("Wq", [H, H]); Wk = din("Wk", [512, H]); Wv = din("Wv", [512, H]); Wo = din("Wo", [H, H])
    Wg = din("Wg", [H, FF]); Wu = din("Wu", [H, FF]); Wd = din("Wd", [FF, H])
    bqc = din("bqc", [128, 16]); bkc = din("bkc", [128, 4]); bvc = din("bvc", [64, 8]); boc = din("boc", [128, 16])
    kwpair = din("kwpair", [2, 16])
    ident = din("ident", [128, 128]); Pi = din("Pi", [128, 128]); E2 = din("E2", [2, 128])
    onesr = din("onesr", [1, 128]); onesc = din("onesc", [128, 1]); ones2 = din("ones2", [128, 2])
    bc65 = din("bc65", [65, 64])
    tri = din("tri", [128, 128])
    invfh = din("invfh", [64, 1]); invfl = din("invfl", [64, 1]); tb = din("tb", [64, 4096])
    nsin = din("nsin", [64, 4096]); ncos = din("ncos", [64, 4096])
    wselr = din("wselr", [1, 2 * NU])
    qgidx = din("qgidx", [128, NU * KV], U32)
    tabidx = din("tabidx", [128, NU + 2], U32)

    y = nc.dram_tensor("y", [1024, H], F32, kind="ExternalOutput").ap()

    def dscr(name, shape):
        if debug:
            return nc.dram_tensor(name, list(shape), F32, kind="ExternalOutput").ap()
        return nc.dram_tensor(name, list(shape), F32).ap()

    qpad = dscr("qpad", [2 * KV * 128, 4 * BLK])
    ksp = dscr("ksp", [NU, 128, 8 * BLK])
    q2d = dscr("q2d", [2, 16, 2, BLK])
    vsp = dscr("vsp", [NU, 128, 4 * 520])
    cosd = dscr("cosd", [128, 4096])
    sind = dscr("sind", [128, 4096])
    xTsp = dscr("xTsp", [128, HC * 1024])
    if debug:
        d_oA = nc.dram_tensor("d_oA", [65, 16384], F32, kind="ExternalOutput").ap()
        d_oB = nc.dram_tensor("d_oB", [65, 16384], F32, kind="ExternalOutput").ap()
        d_ncr = nc.dram_tensor("d_ncr", [1, 32], F32, kind="ExternalOutput").ap()
        d_mq = nc.dram_tensor("d_mq", [1, 32], F32, kind="ExternalOutput").ap()
        d_mk = nc.dram_tensor("d_mk", [1, 8], F32, kind="ExternalOutput").ap()
        d_onT = nc.dram_tensor("d_onT", [128, 16384], F32, kind="ExternalOutput").ap()
        d_y1 = nc.dram_tensor("d_y1", [128, 16384], F32, kind="ExternalOutput").ap()
        d_x1 = nc.dram_tensor("d_x1", [128, 16384], F32, kind="ExternalOutput").ap()
        d_h0 = nc.dram_tensor("d_h0", [128, 512], F32, kind="ExternalOutput").ap()
        d_qt = nc.dram_tensor("d_qt", [128, 1536], F32, kind="ExternalOutput").ap()
        d_cr = nc.dram_tensor("d_cr", [1, 8192], F32, kind="ExternalOutput").ap()
        d_kk = nc.dram_tensor("d_kk", [128, 3072], F32, kind="ExternalOutput").ap()
        d_kn = nc.dram_tensor("d_kn", [NU, 4, 2, 512], F32, kind="ExternalOutput").ap()
        d_o2 = nc.dram_tensor("d_o2", [128, 8192], F32, kind="ExternalOutput").ap()

    P = Prog()

    import contextlib
    _es = contextlib.ExitStack()
    with _es:
        AR1 = _es.enter_context(nc.sbuf_tensor("AR1", [128, 16384], F32))
        AR2 = _es.enter_context(nc.sbuf_tensor("AR2", [128, 16384], F32))
        AR3 = _es.enter_context(nc.sbuf_tensor("AR3", [128, 16384], F32))
        AR4 = _es.enter_context(nc.sbuf_tensor("AR4", [128, 3472], F32))
        IDX = _es.enter_context(nc.sbuf_tensor("IDX", [128, 96], U32))
        PS = _es.enter_context(nc.psum_tensor("PS", [128, 4096], F32))
        _lanes = ["mm", "ev", "sc", "boot", "boot2", "tsp", "qtg", "qz", "xa", "xb", "wq",
                  "q2s", "qsp", "utg", "sp", "pi", "crl", "cw", "ks", "va", "vb", "qa", "qb",
                  "os", "xc", "xd", "xsp", "fw", "fd", "st", "dbg"]
        _semh = {n: _es.enter_context(nc.semaphore("s_" + n)) for n in _lanes}
        block = _es.enter_context(nc.Block())
        sems = _semh

        def pb(i, lo=0, hi=512, p0=0, p1=128):
            return PS[p0:p1, i * 512 + lo:i * 512 + hi]

        A4 = AR4
        identS = lambda: A4[:, 0:128]
        PiS = lambda: A4[:, 128:256]
        triS = lambda: A4[:, 256:384]
        onesrS = lambda: A4[0:1, 384:512]
        onescS = lambda: A4[:, 512:513]
        ones2S = lambda: A4[:, 514:516]
        E2S = lambda: A4[0:2, 516:644]
        kwpS = lambda: A4[0:2, 644:660]
        bqS = lambda: A4[:, 660:676]
        bkS = lambda: A4[:, 676:680]
        bvS = lambda: A4[0:64, 680:688]
        boS = lambda: A4[:, 688:704]
        kwcolS = lambda: A4[:, 704:720]
        ccolS = lambda: A4[:, 720:752]
        wcolS = lambda: A4[:, 752:770]
        Mq2S = lambda: A4[0:1, 770:802]
        Mk2S = lambda: A4[0:1, 802:810]
        tmp2S = lambda: A4[0:2, 810:812]
        c2S = lambda: A4[0:1, 814:846]
        crS = lambda: A4[0:1, 846:878]
        invfhS = lambda: A4[0:64, 878:879]
        invflS = lambda: A4[0:64, 879:880]
        wsS = lambda: A4[0:1, 884:902]
        qcosS = lambda ch: A4[:, 1024 + ch * 1024:1024 + ch * 1024 + 512]
        qsinS = lambda ch: A4[:, 1536 + ch * 1024:1536 + ch * 1024 + 512]
        q2stgS = lambda: AR3[0:2, 8192:8704]
        ucosS = lambda: A4[:, 1024:1536]
        usinS = lambda: A4[:, 1536:2048]

        for (dst, src) in [(identS(), ident), (PiS(), Pi), (triS(), tri),
                           (onesrS(), onesr), (onescS(), onesc), (ones2S(), ones2),
                           (E2S(), E2), (kwpS(), kwpair), (bqS(), bqc), (bkS(), bkc),
                           (bvS(), bvc), (boS(), boc), (wsS(), wselr),
                           (invfhS(), invfh), (invflS(), invfl)]:
            P.do("sync", lambda h, s, a=dst, b=src: h.dma_start(out=a, in_=b), "boot", 16)
        P.do("sync", lambda h, s: h.dma_start(out=IDX[:, 0:NU * KV], in_=qgidx), "boot", 16)
        P.do("sync", lambda h, s: h.dma_start(out=IDX[:, 72:72 + NU + 2], in_=tabidx), "boot", 16)
        tbS = AR2[0:64, 0:4096]
        nsS = AR2[0:64, 4096:8192]
        ncS = AR2[0:64, 8192:12288]
        angS = AR2[0:64, 12288:16384]
        redsS = AR3[0:64, 0:4096]
        redcS = AR3[0:64, 4096:8192]
        cosS = AR3[0:64, 8192:12288]
        sinS_ = AR3[0:64, 12288:16384]
        P.do("sync", lambda h, s: h.dma_start(out=tbS, in_=tb), "boot", 16)
        P.do("sync", lambda h, s: h.dma_start(out=nsS, in_=nsin), "boot", 16)
        P.do("sync", lambda h, s: h.dma_start(out=ncS, in_=ncos), "boot", 16)
        for e in ("vector", "scalar", "tensor"):
            P.wl(e, "boot")
        P.do("vector", lambda h, s: h.tensor_scalar_mul(angS, tbS, invfhS()), "ev")
        P.do("vector", lambda h, s: h.scalar_tensor_tensor(angS, tbS, invflS(), angS, ALU.mult, ALU.add), "ev")
        P.do("vector", lambda h, s: h.scalar_tensor_tensor(redsS, nsS, -C1, angS, ALU.mult, ALU.add), "ev")
        P.do("vector", lambda h, s: h.scalar_tensor_tensor(redsS, nsS, -C2, redsS, ALU.mult, ALU.add), "ev")
        P.do("vector", lambda h, s: h.scalar_tensor_tensor(redcS, ncS, -C1, angS, ALU.mult, ALU.add), "ev")
        P.do("vector", lambda h, s: h.scalar_tensor_tensor(redcS, ncS, -C2, redcS, ALU.mult, ALU.add), "ev")
        P.do("vector", lambda h, s: h.tensor_scalar_add(redcS, redcS, HP1), "ev")
        P.do("vector", lambda h, s: h.tensor_scalar_add(redcS, redcS, HP2), "ev")
        P.w("scalar", "ev", P.c("ev"))
        P.do("scalar", lambda h, s: h.activation(cosS, redcS, AF.Sin), "sc")
        P.do("scalar", lambda h, s: h.activation(sinS_[0:32, :], redsS[0:32, :], AF.Sin, scale=-1.0), "sc")
        P.do("scalar", lambda h, s: h.activation(sinS_[32:64, :], redsS[32:64, :], AF.Sin), "sc")
        s_tab = P.c("sc")
        P.w("sync", "sc", s_tab)
        P.do("sync", lambda h, s: h.dma_start(out=cosd[0:64, :], in_=cosS), "tsp", 16)
        P.do("sync", lambda h, s: h.dma_start(out=cosd[64:128, :], in_=cosS), "tsp", 16)
        P.do("sync", lambda h, s: h.dma_start(out=sind[0:64, :], in_=sinS_), "tsp", 16)
        P.do("sync", lambda h, s: h.dma_start(out=sind[64:128, :], in_=sinS_), "tsp", 16)

        P.bank_w(0)
        P.do("tensor", lambda h, s: h.matmul(PS[0:128, 0:16], E2S(), kwpS(), start=True, stop=True), "mm")
        P.do("tensor", lambda h, s: h.matmul(PS[0:128, 16:34], onesrS(), wsS(), start=True, stop=True), "mm")
        P.bank_wrote(0)
        P.bank_r("vector", 0)
        P.do("vector", lambda h, s: h.tensor_copy(kwcolS(), PS[0:128, 0:16]), "ev")
        P.do("vector", lambda h, s: h.tensor_copy(wcolS(), PS[0:128, 16:34]), "ev")
        P.bank_read("vector", 0)
        P.do("vector", lambda h, s: h.memset(Mk2S(), 0.0), "ev")

        cosd_v = cosd[:].rearrange("p (a b) -> (p a) b", b=512)
        sind_v = sind[:].rearrange("p (a b) -> (p a) b", b=512)
        P.wl("gpsimd", "tsp", "boot")
        for ch in range(2):
            P.do("gpsimd", lambda h, s, ch=ch: h.indirect_dma_start(
                qcosS(ch), None, cosd_v, IndirectOffsetOnAxis(ap=IDX[:, 72 + NU + ch:73 + NU + ch], axis=0)), "qtg", 16)
            P.do("gpsimd", lambda h, s, ch=ch: h.indirect_dma_start(
                qsinS(ch), None, sind_v, IndirectOffsetOnAxis(ap=IDX[:, 72 + NU + ch:73 + NU + ch], axis=0)), "qtg", 16)
        qtab_done = None

        zb = AR1[:, 0:2048]
        P.do("vector", lambda h, s: h.memset(zb, 0.0), "ev")
        P.w("gpsimd", "ev", P.c("ev"))
        for blk_ in range(16):
            P.do("gpsimd", lambda h, s, b=blk_: h.dma_start(out=qpad[b * 128:(b + 1) * 128, :], in_=zb), "qz", 16)
        qpad_zeroed = None

        def strip16(src_fn, dst_fn, banks=(4, 5), evac_eng="vector"):
            for g in range(4):
                b = banks[g % 2]
                P.bank_w(b)
                for j in range(4):
                    P.do("tensor", lambda h, s, src=src_fn(g * 4 + j), b=b, j=j:
                         h.transpose(pb(b, j * 128, (j + 1) * 128), src, identS()), "mm")
                P.bank_wrote(b)
                P.bank_r(evac_eng, b)
                P.do(evac_eng, lambda h, s, dst=dst_fn(g), b=b:
                     (h.tensor_copy(dst, pb(b)) if evac_eng == "vector" else h.copy(dst, pb(b))),
                     ESEM[evac_eng])
                P.bank_read(evac_eng, b)

        xstg = lambda i: AR3[:, (i % 2) * 2048:(i % 2) * 2048 + 2048]
        wraw = AR3[:, 4096:6144]
        wTb = AR3[:, 6144:8192]
        qsS = AR3[:, 10240:10752]
        m1S = AR3[:, 10752:11264]
        m2S = AR3[:, 11264:11776]
        qtS = AR3[:, 11776:12288]
        sqS = AR3[:, 12288:12800]

        P.w("sync", "sc", s_tab)
        P.wl("scalar", "tsp")
        P.wl("vector", "qz")
        xT3 = AR1[:].rearrange("p (hc t) -> p hc t", t=1024)
        x_ld = []
        for tt in range(8):
            P.w("sync", "mm", P.c("mm")) if tt >= 2 else None
            xl_ = "xa" if tt % 2 == 0 else "xb"
            P.do("sync", lambda h, s, tt=tt: h.dma_start(out=xstg(tt), in_=x_own[tt * 128:(tt + 1) * 128, :]), xl_, 16)
            P.wl("tensor", xl_)
            strip16(lambda j, tt=tt: xstg(tt)[:, j * 128:(j + 1) * 128],
                    lambda g, tt=tt: xT3[:, g * 4:(g + 1) * 4, tt * 128:(tt + 1) * 128])
        xT_done_ev = P.c("ev")

        P.wl("vector", "qtg")
        wq_ld = []
        P.do("sync", lambda h, s: h.dma_start(out=wraw, in_=Wq[0:128, :]), "wq", 16)
        wT3 = None
        last_qspill_gq = 0
        last_norm_mm = 0
        last_q_square = 0
        for dqc in range(16):
            P.wl("tensor", "wq")
            strip16(lambda j: wraw[:, j * 128:(j + 1) * 128],
                    lambda g: wTb[:, g * 512:(g + 1) * 512])
            wqT_done = P.c("ev")
            P.w("sync", "mm", P.c("mm"))
            if dqc + 1 < 16:
                P.do("sync", lambda h, s, dqc=dqc: h.dma_start(out=wraw, in_=Wq[(dqc + 1) * 128:(dqc + 2) * 128, :]), "wq", 16)
            for ch in range(2):
                b = ch
                rb = 2 + ch
                P.w("tensor", "ev", wqT_done)
                P.bank_w(b)
                for hc in range(HC):
                    P.do("tensor", lambda h, s, hc=hc, b=b, ch=ch:
                         h.matmul(pb(b), wTb[:, hc * 128:(hc + 1) * 128],
                                  AR1[:, hc * 1024 + ch * 512:hc * 1024 + ch * 512 + 512],
                                  start=(hc == 0), stop=(hc == HC - 1)), "mm")
                P.bank_wrote(b)
                P.bank_r("vector", b)
                P.w("vector", "mm", P.c("mm"))
                P.do("vector", lambda h, s, dqc=dqc, b=b:
                     h.tensor_scalar(qsS, pb(b), bqS()[:, dqc:dqc + 1], kwcolS()[:, dqc:dqc + 1], ALU.add, ALU.mult), "ev")
                P.bank_read("vector", b)
                ev_qs = P.c("ev")
                P.w("tensor", "ev", ev_qs)
                P.bank_w(rb)
                P.do("tensor", lambda h, s, rb=rb: h.matmul(pb(rb), PiS(), qsS, start=True, stop=True), "mm")
                P.bank_wrote(rb)
                mm_rot = P.c("mm")
                P.do("vector", lambda h, s, ch=ch: h.tensor_mul(m1S, qsS, qcosS(ch)), "ev")
                P.bank_r("vector", rb)
                P.do("vector", lambda h, s, rb=rb, ch=ch: h.tensor_mul(m2S, pb(rb), qsinS(ch)), "ev")
                P.bank_read("vector", rb)
                P.wl("vector", "qsp")
                P.w("vector", "sc", last_q_square)
                P.do("vector", lambda h, s: h.tensor_add(qtS, m1S, m2S), "ev")
                ev_qt = P.c("ev")
                P.w("gpsimd", "ev", ev_qt)
                for hh in range(2):
                    hq = 2 * dqc + hh
                    kvh = hq // 4
                    qh = hq % 4
                    par = kvh % 2
                    r0 = ch * 1024 + kvh * 128 + par * 64
                    P.do("gpsimd", lambda h, s, r0=r0, qh=qh, hh=hh:
                         h.dma_start(out=qpad[r0:r0 + 64, qh * 512:(qh + 1) * 512],
                                     in_=qtS[hh * 64:(hh + 1) * 64, :]), "qsp", 16)

                P.w("scalar", "ev", ev_qt)
                P.w("scalar", "mm", last_norm_mm)
                P.do("scalar", lambda h, s: h.activation(sqS, qtS, AF.Square), "sc")
                sc_sq = P.c("sc")
                last_q_square = sc_sq

                P.w("tensor", "sc", sc_sq)
                P.bank_w(6)
                P.do("tensor", lambda h, s: h.matmul(pb(6, 0, 512, 0, 2), ones2S(), sqS, start=True, stop=True), "mm")
                P.bank_wrote(6)
                last_norm_mm = P.c("mm")
                P.bank_r("vector", 6)
                P.wl("vector", "q2s")
                P.do("vector", lambda h, s: h.tensor_copy(q2stgS(), pb(6, 0, 512, 0, 2)), "ev")
                P.bank_read("vector", 6)
                ev_q2 = P.c("ev")
                P.w("sync", "ev", ev_q2)
                P.do("sync", lambda h, s, ch=ch, dqc=dqc: h.dma_start(out=q2d[ch, dqc], in_=q2stgS()), "q2s", 16)



        P.w("sync", "ev", xT_done_ev)
        P.do("sync", lambda h, s: h.dma_start(out=xTsp[:], in_=AR1[:]), "xsp", 16)

        P.barrier()
        WkT3 = AR2[:].rearrange("p (hc q) -> p hc q", q=512)
        for i in range(4):
            P.w("sync", "mm", P.c("mm")) if i > 0 else None
            P.do("sync", lambda h, s, i=i: h.dma_start(out=AR1[:, 0:2048], in_=Wk[i * 128:(i + 1) * 128, :]), "wq", 16)
            P.wl("tensor", "wq")
            strip16(lambda j: AR1[:, j * 128:(j + 1) * 128],
                    lambda g, i=i: WkT3[:, g * 4:(g + 1) * 4, i * 128:(i + 1) * 128])
        for i in range(4):
            P.w("sync", "mm", P.c("mm"))
            P.do("sync", lambda h, s, i=i: h.dma_start(out=AR1[:, 0:2048], in_=Wv[i * 128:(i + 1) * 128, :]), "wq", 16)
            P.wl("tensor", "wq")
            strip16(lambda j: AR1[:, j * 128:(j + 1) * 128],
                    lambda g, i=i: WkT3[:, 16 + g * 4:16 + (g + 1) * 4, i * 128:(i + 1) * 128])
        wkvT_done = P.c("ev")
        WkTs = lambda hc, dkc: AR2[:, hc * 512 + dkc * 128:hc * 512 + (dkc + 1) * 128]
        WvTs = lambda hc: AR2[:, 8192 + hc * 512:8192 + (hc + 1) * 512]

        kscr = AR3[:, 0:512]
        km1 = AR3[:, 512:1024]
        km2 = AR3[:, 1024:1536]
        ktil = AR3[:, 1536:2048]
        ksq = AR3[:, 2048:2560]
        vscr = lambda kt: AR3[:, 2560 + kt * 520:2560 + (kt + 1) * 520]
        vscr3 = lambda kt: AR3[:, 2560 + kt * 520:2560 + (kt + 1) * 520].rearrange("p (kv e) -> p kv e", e=65)
        P.wl("vector", "xa", "xb", "wq")
        for kt in range(4):
            P.do("vector", lambda h, s, kt=kt: h.memset(vscr3(kt)[:, :, 64:65], 1.0), "ev")

        padb = AR3[:, 7200:7712]
        pones = AR3[0:1, 7712:8224]
        P.do("vector", lambda h, s: h.memset(padb, 0.0), "ev")
        P.do("vector", lambda h, s: h.memset(pones, 1.0), "ev")
        ev_pad = P.c("ev")
        P.w("gpsimd", "ev", ev_pad)
        for u_ in range(NU):
            for kv_ in range(KV):
                if kv_ % 2 == 0:
                    P.do("gpsimd", lambda h, s, u_=u_, kv_=kv_: h.dma_start(
                        out=ksp[u_, 64:65, kv_ * BLK:(kv_ + 1) * BLK], in_=pones), "pi", 16)
                    P.do("gpsimd", lambda h, s, u_=u_, kv_=kv_: h.dma_start(
                        out=ksp[u_, 65:128, kv_ * BLK:(kv_ + 1) * BLK], in_=padb[0:63, :]), "pi", 16)
                else:
                    P.do("gpsimd", lambda h, s, u_=u_, kv_=kv_: h.dma_start(
                        out=ksp[u_, 0:63, kv_ * BLK:(kv_ + 1) * BLK], in_=padb[0:63, :]), "pi", 16)
                    P.do("gpsimd", lambda h, s, u_=u_, kv_=kv_: h.dma_start(
                        out=ksp[u_, 63:64, kv_ * BLK:(kv_ + 1) * BLK], in_=pones), "pi", 16)
        xkT = lambda hc: AR1[:, 4096 + hc * 512:4096 + (hc + 1) * 512]
        xkT3 = AR1[:, 4096:12288].rearrange("p (hc t) -> p hc t", t=512)
        last_ksp_gq = 0
        last_vsp_gq = 0
        stage_tr_mark = {0: P.c("mm"), 1: P.c("mm")}
        last_knorm_mm = 0
        last_k_square = 0
        for u in range(NU):
            P.wl("gpsimd", "tsp")
            P.w("gpsimd", "ev", P.c("ev")) if u > 0 else None
            P.do("gpsimd", lambda h, s, u=u: h.indirect_dma_start(
                ucosS(), None, cosd_v, IndirectOffsetOnAxis(ap=IDX[:, 72 + u:73 + u], axis=0)), "utg", 16)
            P.do("gpsimd", lambda h, s, u=u: h.indirect_dma_start(
                usinS(), None, sind_v, IndirectOffsetOnAxis(ap=IDX[:, 72 + u:73 + u], axis=0)), "utg", 16)
            utab = None

            for tt in range(4):
                P.w("sync", "mm", stage_tr_mark.get(tt % 2, 0))
                P.do("sync", lambda h, s, u=u, tt=tt: h.dma_start(
                    out=AR1[:, (tt % 2) * 2048:(tt % 2) * 2048 + 2048],
                    in_=xkeys[u * 512 + tt * 128:u * 512 + (tt + 1) * 128, :]), "xa" if tt % 2 == 0 else "xb", 16)
                P.wl("tensor", "xa" if tt % 2 == 0 else "xb")
                strip16(lambda j, tt=tt: AR1[:, (tt % 2) * 2048 + j * 128:(tt % 2) * 2048 + (j + 1) * 128],
                        lambda g, tt=tt: xkT3[:, g * 4:(g + 1) * 4, tt * 128:(tt + 1) * 128])
                stage_tr_mark[tt % 2] = P.c("mm")
            xkT_ev = P.c("ev")

            P.w("tensor", "ev", xkT_ev)
            P.wl("vector", "utg")
            for dkc in range(4):
                b = dkc % 2
                P.bank_w(b)
                for hc in range(HC):
                    P.do("tensor", lambda h, s, hc=hc, dkc=dkc, b=b:
                         h.matmul(pb(b), WkTs(hc, dkc), xkT(hc), start=(hc == 0), stop=(hc == HC - 1)), "mm")
                P.bank_wrote(b)
                P.bank_r("vector", b)
                P.w("vector", "mm", P.c("mm"))
                P.do("vector", lambda h, s, dkc=dkc, b=b:
                     h.tensor_scalar_add(kscr, pb(b), bkS()[:, dkc:dkc + 1]), "ev")
                P.bank_read("vector", b)
                ev_k = P.c("ev")
                P.w("tensor", "ev", ev_k)
                P.bank_w(2)
                P.do("tensor", lambda h, s: h.matmul(pb(2), PiS(), kscr, start=True, stop=True), "mm")
                P.bank_wrote(2)
                P.do("vector", lambda h, s: h.tensor_mul(km1, kscr, ucosS()), "ev")
                P.bank_r("vector", 2)
                P.do("vector", lambda h, s: h.tensor_mul(km2, pb(2), usinS()), "ev")
                P.bank_read("vector", 2)
                P.wl("vector", "sp")
                P.w("vector", "sc", last_k_square)
                P.do("vector", lambda h, s: h.tensor_add(ktil, km1, km2), "ev")
                ev_kt = P.c("ev")
                P.w("gpsimd", "ev", ev_kt)
                P.do("gpsimd", lambda h, s, u=u, dkc=dkc: h.dma_start(
                    out=ksp[u, 0:64, (2 * dkc) * BLK:(2 * dkc) * BLK + BLK], in_=ktil[0:64, :]), "sp", 16)
                P.do("gpsimd", lambda h, s, u=u, dkc=dkc: h.dma_start(
                    out=ksp[u, 64:128, (2 * dkc + 1) * BLK:(2 * dkc + 1) * BLK + BLK], in_=ktil[64:128, :]), "sp", 16)

                P.w("scalar", "ev", ev_kt)
                P.w("scalar", "mm", last_knorm_mm)
                P.do("scalar", lambda h, s: h.activation(ksq, ktil, AF.Square), "sc")
                sc_k = P.c("sc")
                last_k_square = sc_k
                P.w("tensor", "sc", sc_k)
                P.bank_w(6)
                P.do("tensor", lambda h, s: h.matmul(pb(6, 0, 512, 0, 2), ones2S(), ksq, start=True, stop=True), "mm")
                P.bank_wrote(6)
                P.bank_r("vector", 6)
                P.do("vector", lambda h, s: h.reduce_max(tmp2S()[0:2, 1:2], pb(6, 0, 512, 0, 2), axis=AX.X), "ev")
                P.bank_read("vector", 6)
                P.w("tensor", "ev", P.c("ev"))
                P.bank_w(7)
                P.do("tensor", lambda h, s: h.transpose(pb(7, 0, 2, 0, 1), tmp2S()[0:2, 1:2], identS()[0:2, 0:2]), "mm")
                P.bank_wrote(7)
                last_knorm_mm = P.c("mm")
                P.bank_r("vector", 7)
                P.do("vector", lambda h, s, dkc=dkc: h.tensor_max(Mk2S()[0:1, 2 * dkc:2 * dkc + 2],
                                                                  Mk2S()[0:1, 2 * dkc:2 * dkc + 2],
                                                                  pb(7, 0, 2, 0, 1)), "ev")
                P.drain("vector")
                P.bank_read("vector", 7)
            for kt in range(4):
                b = kt % 2
                P.bank_w(b)
                for hc in range(HC):
                    P.do("tensor", lambda h, s, hc=hc, kt=kt, b=b:
                         h.matmul(pb(b), xkT(hc)[:, kt * 128:(kt + 1) * 128], WvTs(hc),
                                  start=(hc == 0), stop=(hc == HC - 1)), "mm")
                P.bank_wrote(b)
                P.bank_r("vector", b)
                P.wl("vector", "sp")
                P.do("vector", lambda h, s, kt=kt, b=b:
                     h.tensor_copy(vscr3(kt)[:, :, 0:64], pb(b).rearrange("p (kv dv) -> p kv dv", dv=64)), "ev")
                P.bank_read("vector", b)
                ev_v = P.c("ev")
                P.w("gpsimd", "ev", ev_v)
                P.do("gpsimd", lambda h, s, u=u, kt=kt: h.dma_start(out=vsp[u, :, kt * 520:(kt + 1) * 520], in_=vscr(kt)), "sp", 16)



        P.drain("vector")
        crow = lambda: A4[0:1, 1024:3072]
        P.wl("sync", "q2s", "qsp", "utg", "qtg")
        for ch in range(2):
            for kv in range(KV):
                P.w("sync", "ev", P.c("ev"))
                P.w("sync", "sc", P.c("sc"))
                P.wl("sync", "cw")
                P.do("sync", lambda h, s, ch=ch, kv=kv: h.dma_start(
                    out=crow(), in_=q2d[ch, 2 * kv:2 * kv + 2]), "crl", 16)
                P.wl("vector", "crl")
                P.do("vector", lambda h, s, kv=kv: h.tensor_single_scalar(
                    crow(), crow(), Mk2S()[0:1, kv:kv + 1], ALU.mult), "ev")
                P.drain("vector")
                ev_c1 = P.c("ev")
                P.w("scalar", "ev", ev_c1)
                P.do("scalar", lambda h, s: h.activation(crow(), crow(), AF.Sqrt), "sc")
                sc_c = P.c("sc")
                P.w("vector", "sc", sc_c)
                P.do("vector", lambda h, s: h.tensor_scalar_add(crow(), crow(), -70.0), "ev")
                P.drain("vector")
                P.do("vector", lambda h, s: h.tensor_scalar_max(crow(), crow(), 0.0), "ev")
                P.drain("vector")
                P.do("vector", lambda h, s: h.tensor_scalar_mul(crow(), crow(), -1.0), "ev")
                P.drain("vector")
                ev_c2 = P.c("ev")
                P.w("gpsimd", "ev", ev_c2)
                r_c = 64 if kv % 2 == 0 else 63
                P.do("gpsimd", lambda h, s, ch=ch, kv=kv, r_c=r_c: h.dma_start(
                    out=qpad[ch * 1024 + kv * 128 + r_c:ch * 1024 + kv * 128 + r_c + 1, :],
                    in_=crow()), "cw", 16)

        P.barrier()
        Kslot = AR1[:, 0:4096]
        Kc = lambda kv: Kslot[:, kv * 512:(kv + 1) * 512]
        Vslot = lambda i: AR1[:, 4096 + (i % 2) * 2080:4096 + (i % 2) * 2080 + 2080]
        Vc = lambda i, kt: Vslot(i)[:, kt * 520:(kt + 1) * 520]
        qslot = lambda i: AR1[:, 8256 + (i % 2) * 2048:8256 + (i % 2) * 2048 + 2048]
        Pbuf = lambda kt: AR1[:, 12352 + kt * 512:12352 + (kt + 1) * 512]
        oaccA = AR2[0:65, :]
        oaccB = AR3[0:65, :]

        P.do("vector", lambda h, s: h.memset(oaccA, 0.0), "ev")
        P.do("vector", lambda h, s: h.memset(oaccB, 0.0), "ev")
        oacc_init = P.c("ev")

        kspv = lambda u: ksp[u]
        vspv = lambda u: vsp[u]
        P.wl("sync", "sp", "pi", "cw")
        P.do("sync", lambda h, s: h.dma_start(out=Kslot, in_=kspv(0)), "ks", 16)
        P.do("sync", lambda h, s: h.dma_start(out=Vslot(0), in_=vspv(0)), "va", 16)
        qpad_v = qpad[:]
        P.wl("gpsimd", "cw", "qz", "qsp")
        P.do("gpsimd", lambda h, s: h.indirect_dma_start(
            qslot(0), None, qpad_v, IndirectOffsetOnAxis(ap=IDX[:, 0:1], axis=0)), "qa", 16)

        exp_bank_sc = {4: 0, 5: 0, 6: 0, 7: 0}
        omm_of_p = {}
        stt_hist = []
        P.w("tensor", "ev", oacc_init)
        step = 0
        for u in range(NU):
            diag = (u >= 7)
            if u == 7:
                P.w("vector", "mm", P.c("mm"))
                for kt in range(1, 4):
                    P.do("vector", lambda h, s, kt=kt: h.memset(Pbuf(kt)[:, 0:kt * 128], 0.0), "ev")
                pz = P.c("ev")
                P.w("scalar", "ev", pz)
                P.w("tensor", "ev", pz)
            if u > 0:
                P.w("sync", "mm", P.c("mm"))
                P.do("sync", lambda h, s, u=u: h.dma_start(out=Kslot, in_=kspv(u)), "ks", 16)
            if u + 1 < NU:
                P.w("sync", "mm", P.c("mm")) if u > 0 else None
                P.do("sync", lambda h, s, u=u: h.dma_start(out=Vslot(u + 1), in_=vspv(u + 1)),
                     "va" if (u + 1) % 2 == 0 else "vb", 16)
            P.wl("tensor", "va" if u % 2 == 0 else "vb")
            P.wl("tensor", "ks")
            for kv in range(KV):
                qi = u * KV + kv
                if qi + 1 < NU * KV:
                    P.w("gpsimd", "mm", P.c("mm")) if qi >= 1 else None
                    P.do("gpsimd", lambda h, s, qi=qi: h.indirect_dma_start(
                        qslot(qi + 1), None, qpad_v, IndirectOffsetOnAxis(ap=IDX[:, qi + 1:qi + 2], axis=0)),
                        "qa" if (qi + 1) % 2 == 0 else "qb", 16)
                P.wl("tensor", "qa" if qi % 2 == 0 else "qb")
                for qh in range(4):
                    hq = kv * 4 + qh
                    ob = 2 + (step % 2)
                    mm_s_l = []
                    for kt in range(4):
                        sb = 4 + kt
                        P.w("tensor", "sc", exp_bank_sc[sb])
                        P.do("tensor", lambda h, s, kv=kv, qh=qh, kt=kt, sb=sb, qi=qi:
                             h.matmul(pb(sb), Kc(kv)[:, kt * 128:(kt + 1) * 128],
                                      qslot(qi)[:, qh * 512:(qh + 1) * 512],
                                      start=True, stop=True), "mm")
                        mm_s_l.append(P.c("mm"))
                    for kt in range(4):
                        sb = 4 + kt
                        lo = kt * 128 if diag else 0
                        P.w("scalar", "mm", max(mm_s_l[kt], omm_of_p.get(kt, 0)))
                        P.do("scalar", lambda h, s, kt=kt, sb=sb, lo=lo:
                             h.activation(Pbuf(kt)[:, lo:512], pb(sb, lo, 512), AF.Exp), "sc")
                        exp_bank_sc[sb] = P.c("sc")
                        sc_exp = P.c("sc")
                        if diag:
                            P.w("vector", "sc", sc_exp)
                            P.do("vector", lambda h, s, kt=kt, lo=lo:
                                 h.tensor_mul(Pbuf(kt)[:, lo:lo + 128], Pbuf(kt)[:, lo:lo + 128], triS()), "ev")
                            P.w("tensor", "ev", P.c("ev"))
                        else:
                            P.w("tensor", "sc", sc_exp)
                        if kt == 0:
                            if len(stt_hist) >= 2:
                                P.w("tensor", "ev", stt_hist[-2])
                        P.do("tensor", lambda h, s, u=u, kv=kv, kt=kt, ob=ob:
                             h.matmul(pb(ob, 0, 512, 0, 65), Vc(u, kt)[:, kv * 65:(kv + 1) * 65],
                                      Pbuf(kt), start=(kt == 0), stop=(kt == 3)), "mm")
                        omm_of_p[kt] = P.c("mm")
                    mm_o = P.c("mm")
                    P.w("vector", "mm", mm_o)
                    P.do("vector", lambda h, s, u=u, hq=hq, ob=ob:
                         h.scalar_tensor_tensor(oaccA[:, hq * 512:(hq + 1) * 512], pb(ob, 0, 512, 0, 65),
                                                wcolS()[0:65, u:u + 1], oaccA[:, hq * 512:(hq + 1) * 512],
                                                ALU.mult, ALU.add), "ev")
                    P.do("vector", lambda h, s, u=u, hq=hq, ob=ob:
                         h.scalar_tensor_tensor(oaccB[:, hq * 512:(hq + 1) * 512], pb(ob, 0, 512, 0, 65),
                                                wcolS()[0:65, NU + u:NU + u + 1], oaccB[:, hq * 512:(hq + 1) * 512],
                                                ALU.mult, ALU.add), "ev")
                    stt_hist.append(P.c("ev"))
                    step += 1

        P.barrier()
        if debug:
            P.do("sync", lambda h, s: h.dma_start(out=d_oA, in_=oaccA), "dbg", 16)
            P.do("sync", lambda h, s: h.dma_start(out=d_oB, in_=oaccB), "dbg", 16)
            P.do("sync", lambda h, s: h.dma_start(out=d_ncr, in_=crS()), "dbg", 16)
        onT = AR1
        P.do("vector", lambda h, s: h.tensor_scalar_max(oaccA[64:65, :], oaccA[64:65, :], 1e-37), "ev")
        P.do("vector", lambda h, s: h.tensor_scalar_max(oaccB[64:65, :], oaccB[64:65, :], 1e-37), "ev")
        P.drain("vector")
        P.w("tensor", "ev", P.c("ev"))
        bc65S = lambda: A4[0:65, 902:966]
        lbr = lambda: A4[0:64, 966:1478]
        onscr = lambda: A4[0:64, 1478:1990]
        P.do("sync", lambda h, s: h.dma_start(out=bc65S(), in_=bc65), "boot2", 16)
        P.wl("tensor", "boot2")
        last_on_gq = 0
        for ch in range(2):
            oacc = oaccA if ch == 0 else oaccB
            for hq in range(NH):
                kvh = hq // 4
                fc = hq // 2
                hh = hq % 2
                b = hq % 2
                P.bank_w(b)
                P.do("tensor", lambda h, s, oacc=oacc, hq=hq, b=b:
                     h.matmul(pb(b, 0, 512, 0, 64), bc65S(), oacc[:, hq * 512:(hq + 1) * 512], start=True, stop=True), "mm")
                P.bank_wrote(b)
                P.bank_r("vector", b)
                P.w("vector", "mm", P.c("mm"))
                P.do("vector", lambda h, s, b=b: h.reciprocal(lbr(), pb(b, 0, 512, 0, 64)), "ev")
                P.bank_read("vector", b)
                P.wl("vector", "os")
                P.do("vector", lambda h, s, oacc=oacc, hq=hq:
                     h.tensor_mul(onscr(), oacc[0:64, hq * 512:(hq + 1) * 512], lbr()), "ev")
                P.do("vector", lambda h, s, kvh=kvh: h.tensor_scalar_add(onscr(), onscr(), bvS()[:, kvh:kvh + 1]), "ev")
                ev_on = P.c("ev")
                if hh == 0:
                    P.do("vector", lambda h, s, fc=fc, ch=ch: h.tensor_copy(
                        onT[0:64, fc * 1024 + ch * 512:fc * 1024 + ch * 512 + 512], onscr()), "ev")
                    last_on_ev = P.c("ev")
                else:
                    P.w("gpsimd", "ev", ev_on)
                    P.do("gpsimd", lambda h, s, fc=fc, ch=ch: h.dma_start(
                        out=onT[64:128, fc * 1024 + ch * 512:fc * 1024 + ch * 512 + 512], in_=onscr()), "os", 16)
        onT_ev = P.c("ev")

        P.barrier()
        wostg = AR3[:, 0:2048]
        woT = AR3[:, 4096:6144]
        xchunk = lambda i: AR3[:, 8192 + (i % 2) * 512:8192 + (i % 2) * 512 + 512]
        sqs2 = AR3[:, 9216:9728]
        sqs3 = AR3[:, 9728:10240]
        wo_ld = []
        P.wl("sync", "xsp")
        P.do("sync", lambda h, s: h.dma_start(out=wostg, in_=Wo[0:128, :]), "wq", 16)
        y1_hist = []
        for fo in range(16):
            P.wl("tensor", "wq")
            strip16(lambda j: wostg[:, j * 128:(j + 1) * 128],
                    lambda g: woT[:, g * 512:(g + 1) * 512])
            woT_done = P.c("ev")
            P.w("sync", "mm", P.c("mm"))
            if fo + 1 < 16:
                P.do("sync", lambda h, s, fo=fo: h.dma_start(out=wostg, in_=Wo[(fo + 1) * 128:(fo + 2) * 128, :]), "wq", 16)
            for ch in range(2):
                if len(y1_hist) >= 2:
                    P.w("sync", "ev", y1_hist[-2])
                P.do("sync", lambda h, s, fo=fo, ch=ch: h.dma_start(
                    out=xchunk(ch), in_=xTsp[:, fo * 1024 + ch * 512:fo * 1024 + ch * 512 + 512]), "xc" if ch == 0 else "xd", 16)
                b = ch
                P.w("tensor", "ev", woT_done)
                P.bank_w(b)
                for fi in range(16):
                    P.do("tensor", lambda h, s, fo=fo, fi=fi, ch=ch, b=b:
                         h.matmul(pb(b), woT[:, fi * 128:(fi + 1) * 128],
                                  onT[:, fi * 1024 + ch * 512:fi * 1024 + ch * 512 + 512],
                                  start=(fi == 0), stop=(fi == 15)), "mm")
                P.bank_wrote(b)
                P.bank_r("vector", b)
                P.wl("vector", "xc" if ch == 0 else "xd")
                P.do("vector", lambda h, s, fo=fo, ch=ch, b=b:
                     h.scalar_tensor_tensor(AR2[:, fo * 1024 + ch * 512:fo * 1024 + ch * 512 + 512],
                                            pb(b), boS()[:, fo:fo + 1], xchunk(ch),
                                            ALU.add, ALU.add), "ev")
                P.bank_read("vector", b)
                y1_hist.append(P.c("ev"))
        y1_done = P.c("ev")
        if debug:
            P.w("sync", "ev", y1_done)
            P.do("sync", lambda h, s: h.dma_start(out=d_onT, in_=onT[:]), "dbg", 16)
            P.do("sync", lambda h, s: h.dma_start(out=d_y1, in_=AR2[:]), "dbg", 16)
            P.wl("tensor", "dbg")
            P.wl("vector", "dbg")
            P.wl("scalar", "dbg")

        murow = lambda: A4[0:1, 1024:1536]
        isrow = lambda: A4[0:1, 1536:2048]
        r1row = lambda: A4[0:1, 2048:2560]
        P.w("tensor", "ev", y1_done)
        for ch in range(2):
            P.bank_w(2)
            for hc in range(HC):
                P.do("tensor", lambda h, s, hc=hc, ch=ch: h.matmul(pb(2, 0, 512, 0, 1), onescS(),
                                                                   AR2[:, hc * 1024 + ch * 512:hc * 1024 + ch * 512 + 512],
                                                                   start=(hc == 0), stop=(hc == HC - 1)), "mm")
            P.bank_wrote(2)
            last_sq_mm = 0
            P.bank_w(3)
            ev_now = P.c("ev")
            for hc in range(HC):
                P.w("scalar", "ev", ev_now)
                P.w("scalar", "mm", last_sq_mm)
                P.do("scalar", lambda h, s, hc=hc, ch=ch: h.activation(sqs2, AR2[:, hc * 1024 + ch * 512:hc * 1024 + ch * 512 + 512], AF.Square), "sc")
                sc_s = P.c("sc")
                P.w("tensor", "sc", sc_s)
                P.do("tensor", lambda h, s, hc=hc: h.matmul(pb(3, 0, 512, 0, 1), onescS(), sqs2,
                                                            start=(hc == 0), stop=(hc == HC - 1)), "mm")
                last_sq_mm = P.c("mm")
            P.bank_wrote(3)
            P.bank_r("vector", 2)
            P.bank_r("vector", 3)
            P.do("vector", lambda h, s: h.tensor_scalar_mul(murow(), pb(2, 0, 512, 0, 1), 1.0 / H), "ev")
            P.do("vector", lambda h, s: h.tensor_mul(r1row(), murow(), murow()), "ev")
            P.do("vector", lambda h, s: h.scalar_tensor_tensor(isrow(), pb(3, 0, 512, 0, 1), 1.0 / H, r1row(), ALU.mult, ALU.subtract), "ev")
            P.bank_read("vector", 2)
            P.bank_read("vector", 3)
            P.do("vector", lambda h, s: h.tensor_scalar_add(isrow(), isrow(), 1e-5), "ev")
            P.w("scalar", "ev", P.c("ev"))
            P.do("scalar", lambda h, s: h.activation(isrow(), isrow(), AF.Sqrt), "sc")
            P.w("vector", "sc", P.c("sc"))
            P.do("vector", lambda h, s: h.reciprocal(isrow(), isrow()), "ev")
            P.w("tensor", "ev", P.c("ev"))
            P.bank_w(2); P.bank_w(3)
            P.do("tensor", lambda h, s: h.matmul(pb(2), onesrS(), murow(), start=True, stop=True), "mm")
            P.do("tensor", lambda h, s: h.matmul(pb(3), onesrS(), isrow(), start=True, stop=True), "mm")
            P.bank_wrote(2); P.bank_wrote(3)
            P.bank_r("vector", 2); P.bank_r("vector", 3)
            for hc in range(HC):
                P.do("vector", lambda h, s, hc=hc, ch=ch: h.tensor_sub(sqs3, AR2[:, hc * 1024 + ch * 512:hc * 1024 + ch * 512 + 512], pb(2)), "ev")
                P.do("vector", lambda h, s, hc=hc, ch=ch: h.tensor_mul(AR2[:, hc * 1024 + ch * 512:hc * 1024 + ch * 512 + 512], sqs3, pb(3)), "ev")
            P.bank_read("vector", 2); P.bank_read("vector", 3)
        x1_done = P.c("ev")
        if debug:
            P.w("sync", "ev", x1_done)
            P.do("sync", lambda h, s: h.dma_start(out=d_x1, in_=AR2[:]), "dbg", 16)

        P.barrier()
        gsS = lambda: A4[:, 1024:1536]
        hbS = lambda: A4[:, 1536:2048]
        AmO = [128, 256, 384, 516]
        AmF = lambda i: A4[:, AmO[i]:AmO[i] + 128]
        geS = lambda i: A4[:, 2560 + i * 128:2560 + (i + 1) * 128]
        cs0 = lambda: A4[:, 3328:3456]
        stat = lambda i: A4[:, 3456 + i:3457 + i]
        hsT = lambda: A4[:, 2048:2560]
        out2 = AR1[:, 8192:16384]
        Wdb = AR1[:, 0:8192]
        Wgb = AR3[:, 0:8192]
        Wub = AR3[:, 8192:16384]

        for half in range(2):
            ch = half
            P.w("vector", "mm", P.c("mm"))
            P.wl("vector", "os", "st")
            P.do("vector", lambda h, s: h.memset(out2, 0.0), "ev")
            out2_init = P.c("ev")
            if half == 0:
                ld_gu = {}
                ld_d = {}
            def load_gu(ib):
                P.w("sync", "mm", P.c("mm"))
                for hc2 in range(HC):
                    P.do("sync", lambda h, s, ib=ib, hc2=hc2: h.dma_start(
                        out=Wgb[:, hc2 * 512:(hc2 + 1) * 512],
                        in_=Wg[hc2 * 128:(hc2 + 1) * 128, ib * 512:(ib + 1) * 512]), "fw", 16)
                    P.do("sync", lambda h, s, ib=ib, hc2=hc2: h.dma_start(
                        out=Wub[:, hc2 * 512:(hc2 + 1) * 512],
                        in_=Wu[hc2 * 128:(hc2 + 1) * 128, ib * 512:(ib + 1) * 512]), "fw", 16)
                ld_gu[ib] = P.c("fw")
            def load_d(ib):
                P.w("sync", "mm", P.c("mm"))
                for ic in range(4):
                    P.do("sync", lambda h, s, ib=ib, ic=ic: h.dma_start(
                        out=Wdb[:, ic * 2048:(ic + 1) * 2048],
                        in_=Wd[ib * 512 + ic * 128:ib * 512 + (ic + 1) * 128, :]), "fd", 16)
                ld_d[ib] = P.c("fd")
            if half == 0:
                load_gu(0)
                load_d(0)
            pend = None
            tr_mark = 0
            def tail_head(pd):
                nonlocal tr_mark
                P.w("tensor", "ev", pd[2])
                P.bank_w(2)
                for ic in range(4):
                    P.do("tensor", lambda h, s, ic=ic: h.transpose(pb(2, ic * 128, (ic + 1) * 128), hbS()[:, ic * 128:(ic + 1) * 128], identS()), "mm")
                P.bank_wrote(2)
                tr_mark = P.c("mm")
                P.bank_r("vector", 2)
                P.do("vector", lambda h, s: h.tensor_copy(hsT(), pb(2)), "ev")
                P.bank_read("vector", 2)
                return P.c("ev")
            def tail_mms(pd, ev_hsT):
                P.w("tensor", "fd", ld_d[pd[0]])
                P.w("tensor", "ev", ev_hsT)
                for fb in range(4):
                    b = [3, 4, 6, 7][fb]
                    P.bank_w(b)
                    for ic in range(4):
                        P.do("tensor", lambda h, s, ic=ic, fb=fb, b=b:
                             h.matmul(pb(b), hsT()[:, ic * 128:(ic + 1) * 128],
                                      Wdb[:, ic * 2048 + fb * 512:ic * 2048 + (fb + 1) * 512],
                                      start=(ic == 0), stop=(ic == 3)), "mm")
                    P.bank_wrote(b)
                if pd[1] == 3 and pd[0] + 1 < 16:
                    load_d(pd[0] + 1)
            def tail_evacs(pd):
                for fb in range(4):
                    b = [3, 4, 6, 7][fb]
                    P.bank_r("vector", b)
                    P.do("vector", lambda h, s, ttp=pd[1], fb=fb, b=b:
                         h.tensor_add(out2[:, ttp * 2048 + fb * 512:ttp * 2048 + (fb + 1) * 512],
                                      out2[:, ttp * 2048 + fb * 512:ttp * 2048 + (fb + 1) * 512],
                                      pb(b)), "ev")
                    P.bank_read("vector", b)
            ev_mult_prev = 0
            for ib in range(16):
                for tt in range(4):
                    x1s = lambda hc2, tt=tt, ch=ch: AR2[:, hc2 * 1024 + ch * 512 + tt * 128:hc2 * 1024 + ch * 512 + tt * 128 + 128]
                    hs_ev = None
                    if pend is not None:
                        hs_ev = tail_head(pend)
                    if tt == 0:
                        P.w("tensor", "fw", ld_gu[ib])
                    P.bank_w(0)
                    for hc2 in range(HC):
                        P.do("tensor", lambda h, s, hc2=hc2, x1s=x1s:
                             h.matmul(pb(0), x1s(hc2), Wgb[:, hc2 * 512:(hc2 + 1) * 512],
                                      start=(hc2 == 0), stop=(hc2 == HC - 1)), "mm")
                    P.bank_wrote(0)
                    mm_g = P.c("mm")
                    P.bank_w(1)
                    for hc2 in range(HC):
                        P.do("tensor", lambda h, s, hc2=hc2, x1s=x1s:
                             h.matmul(pb(1), x1s(hc2), Wub[:, hc2 * 512:(hc2 + 1) * 512],
                                      start=(hc2 == 0), stop=(hc2 == HC - 1)), "mm")
                    P.bank_wrote(1)
                    mm_u = P.c("mm")
                    if tt == 3 and ib + 1 < 16:
                        load_gu(ib + 1)
                    if pend is not None:
                        tail_mms(pend, hs_ev)
                    P.bank_r("scalar", 0)
                    P.w("scalar", "ev", ev_mult_prev)
                    P.do("scalar", lambda h, s: h.activation(gsS(), pb(0), AF.Silu), "sc")
                    P.bank_read("scalar", 0)
                    sc_g = P.c("sc")
                    P.bank_r("vector", 1)
                    P.w("vector", "sc", sc_g)
                    P.w("vector", "mm", tr_mark)
                    P.do("vector", lambda h, s: h.tensor_mul(hbS(), gsS(), pb(1)), "ev")
                    P.bank_read("vector", 1)
                    ev_h = P.c("ev")
                    ev_mult_prev = ev_h
                    P.w("scalar", "ev", ev_h)
                    hb3 = hbS().rearrange("p (g f) -> p g f", f=4)
                    for f_ in range(4):
                        P.do("scalar", lambda h, s, f_=f_: h.activation(AmF(f_), hb3[:, :, f_:f_ + 1], AF.Abs), "sc")
                    sc_a = P.c("sc")
                    P.w("vector", "sc", sc_a)
                    Am = AmF
                    hv = lambda i: hb3[:, :, i:i + 1]
                    pairs = [(0, 1), (0, 2), (0, 3), (1, 2), (1, 3), (2, 3)]
                    for pi_, (a, b_) in enumerate(pairs):
                        P.do("vector", lambda h, s, pi_=pi_, a=a, b_=b_: h.tensor_tensor(geS(pi_), Am(a), Am(b_), ALU.is_ge), "ev")
                    P.do("vector", lambda h, s: h.tensor_add(cs0(), geS(0), geS(1)), "ev")
                    P.do("vector", lambda h, s: h.tensor_add(cs0(), cs0(), geS(2)), "ev")
                    P.do("vector", lambda h, s: h.scalar_tensor_tensor(hv(0), cs0().rearrange("p (g o) -> p g o", o=1), 2.0, hv(0), ALU.is_ge, ALU.mult), "ev")
                    P.do("vector", lambda h, s: h.tensor_add(cs0(), geS(3), geS(4)), "ev")
                    P.do("vector", lambda h, s: h.tensor_sub(cs0(), cs0(), geS(0)), "ev")
                    P.do("vector", lambda h, s: h.scalar_tensor_tensor(hv(1), cs0().rearrange("p (g o) -> p g o", o=1), 1.0, hv(1), ALU.is_ge, ALU.mult), "ev")
                    P.do("vector", lambda h, s: h.tensor_sub(cs0(), geS(5), geS(1)), "ev")
                    P.do("vector", lambda h, s: h.tensor_sub(cs0(), cs0(), geS(3)), "ev")
                    P.do("vector", lambda h, s: h.scalar_tensor_tensor(hv(2), cs0().rearrange("p (g o) -> p g o", o=1), 0.0, hv(2), ALU.is_ge, ALU.mult), "ev")
                    P.do("vector", lambda h, s: h.tensor_add(cs0(), geS(2), geS(4)), "ev")
                    P.do("vector", lambda h, s: h.tensor_add(cs0(), cs0(), geS(5)), "ev")
                    P.do("vector", lambda h, s: h.scalar_tensor_tensor(hv(3), cs0().rearrange("p (g o) -> p g o", o=1), 1.0, hv(3), ALU.is_le, ALU.mult), "ev")
                    ev_mask = P.c("ev")
                    if debug and half == 0 and ib == 0 and tt == 0:
                        P.w("sync", "ev", ev_mask)
                        P.do("sync", lambda h, s: h.dma_start(out=d_h0, in_=hbS()), "dbg", 16)
                        P.wl("vector", "dbg")
                        P.wl("scalar", "dbg")
                    if pend is not None:
                        tail_evacs(pend)
                    pend = (ib, tt, ev_mask)
            hs_ev = tail_head(pend)
            tail_mms(pend, hs_ev)
            if half == 0:
                load_gu(0)
                load_d(0)
            tail_evacs(pend)
            pend = None
            for tt in range(4):
                for hc2 in range(HC):
                    b = 4 + hc2 % 2
                    P.bank_w(b)
                    P.do("tensor", lambda h, s, hc2=hc2, tt=tt, ch=ch, b=b:
                         h.transpose(pb(b, 0, 128), AR2[:, hc2 * 1024 + ch * 512 + tt * 128:hc2 * 1024 + ch * 512 + tt * 128 + 128], identS()), "mm")
                    P.bank_wrote(b)
                    P.bank_r("vector", b)
                    P.do("vector", lambda h, s, hc2=hc2, tt=tt, b=b:
                         h.tensor_add(out2[:, tt * 2048 + hc2 * 128:tt * 2048 + (hc2 + 1) * 128],
                                      out2[:, tt * 2048 + hc2 * 128:tt * 2048 + (hc2 + 1) * 128],
                                      pb(b, 0, 128)), "ev")
                    P.bank_read("vector", b)
                P.do("vector", lambda h, s, tt=tt: h.reduce_sum(stat(0), out2[:, tt * 2048:(tt + 1) * 2048], axis=AX.X), "ev")
                ev_sum = P.c("ev")
                P.w("scalar", "ev", ev_sum)
                for q4 in range(4):
                    P.do("scalar", lambda h, s, tt=tt, q4=q4: h.activation(
                        gsS(), out2[:, tt * 2048 + q4 * 512:tt * 2048 + (q4 + 1) * 512], AF.Square,
                        accum_out=stat(1 + q4)), "sc")
                sc_ss = P.c("sc")
                P.w("vector", "sc", sc_ss)
                P.do("vector", lambda h, s: h.reduce_sum(stat(5), A4[:, 3457:3461], axis=AX.X), "ev")
                P.drain("vector")
                P.do("vector", lambda h, s: h.tensor_scalar_mul(stat(6), stat(0), 1.0 / H), "ev")
                P.drain("vector")
                P.do("vector", lambda h, s: h.tensor_mul(stat(7), stat(6), stat(6)), "ev")
                P.drain("vector")
                P.do("vector", lambda h, s: h.scalar_tensor_tensor(stat(8), stat(5), 1.0 / H, stat(7), ALU.mult, ALU.subtract), "ev")
                P.drain("vector")
                P.do("vector", lambda h, s: h.tensor_scalar_add(stat(8), stat(8), 1e-5), "ev")
                P.w("scalar", "ev", P.c("ev"))
                P.do("scalar", lambda h, s: h.activation(stat(9), stat(8), AF.Sqrt), "sc")
                P.w("vector", "sc", P.c("sc"))
                P.do("vector", lambda h, s: h.reciprocal(stat(10), stat(9)), "ev")
                P.drain("vector")
                P.do("vector", lambda h, s, tt=tt: h.tensor_scalar(out2[:, tt * 2048:(tt + 1) * 2048],
                                                                   out2[:, tt * 2048:(tt + 1) * 2048],
                                                                   stat(6), stat(10),
                                                                   ALU.subtract, ALU.mult), "ev")
                ev_out = P.c("ev")
                P.w("sync", "ev", ev_out)
                P.do("sync", lambda h, s, tt=tt, half=half: h.dma_start(
                    out=y[half * 512 + tt * 128:half * 512 + (tt + 1) * 128, :],
                    in_=out2[:, tt * 2048:(tt + 1) * 2048]), "st", 16)

        for ln_ in ("st", "q2s", "qsp", "sp", "cw", "os", "xsp", "dbg"):
            P.wl("sync", ln_)

        @block.sync
        def _(h):
            for fn in P.ops["sync"]:
                fn(h, sems)

        @block.tensor
        def _(h):
            for fn in P.ops["tensor"]:
                fn(h, sems)

        @block.vector
        def _(h):
            for fn in P.ops["vector"]:
                fn(h, sems)

        @block.scalar
        def _(h):
            for fn in P.ops["scalar"]:
                fn(h, sems)

        @block.gpsimd
        def _(h):
            for fn in P.ops["gpsimd"]:
                fn(h, sems)

    return nc


def host_inputs(core, inp):
    c = core
    cA, cB = c, 7 - c
    x = np.asarray(inp["L0_attn_x"], dtype=np.float32)
    b0 = [x[0, k * 512:(k + 1) * 512] for k in range(8)]
    b1 = [x[1, k * 512:(k + 1) * 512] for k in range(8)]
    x_own = np.concatenate([b0[cA], b1[cB]], axis=0)
    pre = [b0[k] for k in range(cA)] + [b1[k] for k in range(cB)] + [b0[cA], b1[cB]]
    xkeys = np.concatenate(pre, axis=0)
    unit_pos = list(range(cA)) + list(range(cB)) + [cA, cB]
    unit_sel = [0] * cA + [1] * cB + [0, 1]

    wselr = np.zeros((1, 18), np.float32)
    for u in range(9):
        wselr[0, u] = 1.0 if unit_sel[u] == 0 else 0.0
        wselr[0, 9 + u] = 1.0 - wselr[0, u]

    p_ = np.arange(128, dtype=np.uint32)
    qgidx = np.zeros((128, 72), np.uint32)
    for u in range(9):
        for kv in range(8):
            qgidx[:, u * 8 + kv] = unit_sel[u] * 1024 + kv * 128 + p_
    tabidx = np.zeros((128, 11), np.uint32)
    for u in range(9):
        tabidx[:, u] = p_ * 8 + unit_pos[u]
    tabidx[:, 9] = p_ * 8 + cA
    tabidx[:, 10] = p_ * 8 + cB

    invf = 10000.0 ** (-(np.arange(32, dtype=np.float64) * 2) / 64.0)
    t64 = np.arange(4096, dtype=np.float64)
    ang = invf[:, None] * t64[None, :]
    nsin32 = np.round(ang / (2 * np.pi))
    ncos32 = np.round((ang + np.pi / 2) / (2 * np.pi))
    nsin = np.concatenate([nsin32, nsin32], 0).astype(np.float32)
    ncos = np.concatenate([ncos32, ncos32], 0).astype(np.float32)
    tb = np.tile(np.arange(4096, dtype=np.float32)[None, :], (64, 1))
    invf64_2 = np.concatenate([invf, invf])
    invfh = invf64_2.astype(np.float32)
    invfl = (invf64_2 - invfh.astype(np.float64)).astype(np.float32)
    invfh = invfh.reshape(64, 1); invfl = invfl.reshape(64, 1)

    ident = np.eye(128, dtype=np.float32)
    Pi = np.zeros((128, 128), np.float32)
    for i in range(128):
        g = (i // 64) * 64
        Pi[i, g + (i - g + 32) % 64] = 1.0
    E2 = np.zeros((2, 128), np.float32)
    E2[0, 0:64] = SCALE
    E2[1, 64:128] = SCALE
    onesr = np.ones((1, 128), np.float32)
    onesc = np.ones((128, 1), np.float32)
    ones2 = np.zeros((128, 2), np.float32)
    ones2[0:64, 0] = 1.0
    ones2[64:128, 1] = 1.0
    tri = (np.arange(128)[:, None] <= np.arange(128)[None, :]).astype(np.float32)
    bc65 = np.zeros((65, 64), np.float32)
    bc65[64, :] = 1.0

    return {
        "x_own": np.ascontiguousarray(x_own),
        "xkeys": np.ascontiguousarray(xkeys),
        "Wq": np.asarray(inp["L0_attn_Wq"], np.float32),
        "Wk": np.asarray(inp["L0_attn_Wk"], np.float32),
        "Wv": np.asarray(inp["L0_attn_Wv"], np.float32),
        "Wo": np.asarray(inp["L0_attn_Wo"], np.float32),
        "Wg": np.asarray(inp["L0_ffn_w_gate"], np.float32),
        "Wu": np.asarray(inp["L0_ffn_w_up"], np.float32),
        "Wd": np.asarray(inp["L0_ffn_w_down"], np.float32),
        "bqc": np.ascontiguousarray(np.asarray(inp["L0_attn_bq"], np.float32).reshape(16, 128).T),
        "bkc": np.ascontiguousarray(np.asarray(inp["L0_attn_bk"], np.float32).reshape(4, 128).T),
        "bvc": np.ascontiguousarray(np.asarray(inp["L0_attn_bv"], np.float32).reshape(8, 64).T),
        "boc": np.ascontiguousarray(np.asarray(inp["L0_attn_bo"], np.float32).reshape(16, 128).T),
        "kwpair": np.ascontiguousarray(np.asarray(inp["L0_attn_key_weights"], np.float32).reshape(16, 2).T),
        "ident": ident, "Pi": Pi, "E2": E2, "onesr": onesr, "onesc": onesc, "ones2": ones2,
        "tri": tri, "bc65": bc65, "invfh": invfh, "invfl": invfl, "tb": tb, "nsin": nsin, "ncos": ncos,
        "wselr": wselr, "qgidx": qgidx, "tabidx": tabidx,
    }


def kernel(**inp):
    nc = build_program(debug=False)
    in_maps = [host_inputs(c, inp) for c in range(8)]
    res = run_bass_kernel_spmd(nc, in_maps, core_ids=list(range(8)))
    out = np.empty((2, 4096, 2048), np.float32)
    for c in range(8):
        yv = np.asarray(res.results[c]["y"])
        out[0, c * 512:(c + 1) * 512] = yv[0:512]
        out[1, (7 - c) * 512:(8 - c) * 512] = yv[512:1024]
    return out
